# version 06
# baseline (speedup 1.0000x reference)
import os
import numpy as np
from contextlib import ExitStack

import concourse.bass as bass
import concourse.mybir as mybir
from concourse.bass_utils import run_bass_kernel_spmd

F32 = mybir.dt.float32
BF16 = mybir.dt.bfloat16
AF = mybir.ActivationFunctionType
ALU = mybir.AluOpType
AXX = mybir.AxisListType.X

MAGIC = 12582912.0
NEG = -1.0e9
NEG_PAD = -1.0e30
D = 1024
S = 4096
OWN = 2048
NPAIR = 8
I3 = 3072
NCC = 24
QMAX = 32767.0
EPS = 1e-5


class Ctr:
    __slots__ = ("sem", "n")

    def __init__(self, sem):
        self.sem = sem
        self.n = 0


_PHASES = int(os.environ.get("K_PHASES", "9"))
_SUB = int(os.environ.get("K_SUB", "9"))


def build_nc():
    nc = bass.Bass("TRN2", target_bir_lowering=False,
                   detect_race_conditions=False)

    xT = nc.dram_tensor("xT", [D, S], F32, kind="ExternalInput")
    xoT = nc.dram_tensor("xoT", [D, OWN], F32, kind="ExternalInput")
    wqkv = nc.dram_tensor("wqkv", [D, 3 * D], F32, kind="ExternalInput")
    wmod = nc.dram_tensor("wmod", [D, 6 * D], F32, kind="ExternalInput")
    cvec = nc.dram_tensor("cvec", [128, 8], F32, kind="ExternalInput")
    wup_n = nc.dram_tensor("wup_n", [I3, D], F32, kind="ExternalInput")
    wgate_n = nc.dram_tensor("wgate_n", [I3, D], F32, kind="ExternalInput")
    wdn_n = nc.dram_tensor("wdn_n", [D, I3], F32, kind="ExternalInput")
    dmask = nc.dram_tensor("dmask", [128, 64], F32, kind="ExternalInput")
    eshift = nc.dram_tensor("eshift", [64, 128], F32, kind="ExternalInput")
    ident = nc.dram_tensor("ident", [128, 128], F32, kind="ExternalInput")
    out = nc.dram_tensor("out", [D, OWN], F32, kind="ExternalOutput")

    mod_dram = nc.dram_tensor("mod_dram", [1, 6 * D], F32, kind="Internal")
    h1f_dram = nc.dram_tensor("h1f_dram", [128, 8 * S], BF16, kind="Internal")
    x2_dram = nc.dram_tensor("x2_dram", [128, 8 * OWN], BF16, kind="Internal")
    q_dram = nc.dram_tensor("q_dram", [128, NPAIR * OWN], BF16, kind="Internal")
    upf_dram = nc.dram_tensor("upf_dram", [I3, OWN], BF16, kind="Internal")
    wupq_dram = nc.dram_tensor("wupq_dram", [I3, D], BF16, kind="Internal")
    wgq_dram = nc.dram_tensor("wgq_dram", [I3, D], BF16, kind="Internal")
    wdnq_dram = nc.dram_tensor("wdnq_dram", [D, I3], BF16, kind="Internal")
    scal_dram = nc.dram_tensor("scal_dram", [1, 16], F32, kind="Internal")
    bnc_dram = nc.dram_tensor("bnc_dram", [1, 128], F32, kind="Internal")
    cc_in = [nc.dram_tensor(f"cc_in{i}", [1, 8], F32, kind="Internal") for i in range(3)]
    cc_out = [nc.dram_tensor(f"cc_out{i}", [1, 8], F32, kind="Internal") for i in range(3)]

    es = ExitStack()
    E = es.enter_context

    TM = Ctr(E(nc.semaphore("TM")))
    VC = Ctr(E(nc.semaphore("VC")))
    SA = Ctr(E(nc.semaphore("SA")))
    GC = Ctr(E(nc.semaphore("GC")))
    CCs = Ctr(E(nc.semaphore("CCs")))
    PB = Ctr(E(nc.semaphore("PB")))

    def mkds(name, k=2):
        return [Ctr(E(nc.semaphore(f"{name}{i}"))) for i in range(k)]

    s_cons = mkds("s_cons", 1)[0]
    s_wmod = mkds("s_wmod", 1)[0]
    s_util = mkds("s_util", 1)[0]
    g_es = mkds("g_es", 1)[0]
    g_ar = mkds("g_ar", 1)[0]
    s_h1sp = mkds("s_h1sp")
    s_qsp = mkds("s_qsp")
    g_x = mkds("g_x")
    g_w = mkds("g_w")
    g_wk = mkds("g_wk")
    s_h1ld = mkds("s_h1ld")
    s_qpld = mkds("s_qpld")
    s_xot = mkds("s_xot")
    s_x2st = mkds("s_x2st")
    s_dqld = mkds("s_dqld")
    s_dqst = mkds("s_dqst")
    s_bias = mkds("s_bias")
    s_x2ld = mkds("s_x2ld")
    s_upst = mkds("s_upst")
    s_upld = mkds("s_upld")
    s_x2d = mkds("s_x2d")
    s_out = mkds("s_out")
    c_w = mkds("c_w")
    c_wd = mkds("c_wd", 1)[0]

    def dinc(e_ins, ds):
        e_ins.then_inc(ds.sem, 16)
        ds.n += 16
        return (ds.sem, ds.n)

    def dwait(e, tok):
        e.wait_ge(tok[0], tok[1])

    def recip1(e, dst, srcp):
        e.drain()
        e.tensor_copy(tinyB[0:1, 120:121], srcp)
        e.tensor_copy(tinyB[0:1, 121:122], srcp)
        e.drain()
        e.reciprocal(tinyB[0:1, 122:124], tinyB[0:1, 120:122])
        e.drain()
        return e.tensor_copy(dst, tinyB[0:1, 122:123])

    pA = E(nc.psum_tensor("pA", [128, 512], F32))
    pB = E(nc.psum_tensor("pB", [128, 512], F32))
    pC = E(nc.psum_tensor("pC", [128, 512], F32))
    pD = E(nc.psum_tensor("pD", [128, 512], F32))
    ring0 = E(nc.psum_tensor("ring0", [128, 1024], F32))
    ring1 = E(nc.psum_tensor("ring1", [128, 1024], F32))

    ones_bf = E(nc.sbuf_tensor("ones_bf", [128, 128], BF16))
    ones_f = E(nc.sbuf_tensor("ones_f", [1, 64], F32))
    dmask_sb = E(nc.sbuf_tensor("dmask_sb", [128, 64], F32))
    eshift_sb = E(nc.sbuf_tensor("eshift_sb", [64, 128], BF16))
    ident_sb = E(nc.sbuf_tensor("ident_sb", [128, 128], BF16))
    ca1_sb = E(nc.sbuf_tensor("ca1_sb", [128, 8], F32))
    sa_sb = E(nc.sbuf_tensor("sa_sb", [128, 8], F32))
    ga_sb = E(nc.sbuf_tensor("ga_sb", [128, 8], F32))
    gaup_sb = E(nc.sbuf_tensor("gaup_sb", [64, 8], F32))
    cb1_sb = E(nc.sbuf_tensor("cb1_sb", [128, 8], F32))
    sbv_sb = E(nc.sbuf_tensor("sbv_sb", [128, 8], F32))
    gb_sb = E(nc.sbuf_tensor("gb_sb", [128, 8], F32))
    gbs4_sb = E(nc.sbuf_tensor("gbs4_sb", [128, 8], F32))
    silu_sb = E(nc.sbuf_tensor("silu_sb", [128, 8], F32))
    bias_sb = E(nc.sbuf_tensor("bias_sb", [128, 2], F32))
    scv_sb = E(nc.sbuf_tensor("scv_sb", [128, 16], F32))
    tiny = E(nc.sbuf_tensor("tiny", [1, 16], F32))
    tinyB = E(nc.sbuf_tensor("tinyB", [1, 128], F32))
    dbg_sb = E(nc.sbuf_tensor("dbg_sb", [1, 64], F32))
    macc = E(nc.sbuf_tensor("macc", [128, 8], F32))
    eps_sb = E(nc.sbuf_tensor("eps_sb", [128, 1], F32))
    mgc_sb = E(nc.sbuf_tensor("mgc_sb", [128, 1], F32))
    nmg_sb = E(nc.sbuf_tensor("nmg_sb", [128, 1], F32))

    block = E(nc.Block())

    def barrier():
        tgt = PB.n + 5
        for dec in (block.sync, block.gpsimd, block.tensor, block.vector, block.scalar):
            @dec
            def _(e, tgt=tgt):
                e.drain().then_inc(PB.sem, 1)
                e.wait_ge(PB.sem, tgt)
        PB.n = tgt

    def part_max(col, dst, wait_vc):
        @block.sync
        def _(e, col=col, wv=wait_vc):
            e.wait_ge(VC.sem, wv)
            dinc(e.dma_start(out=bnc_dram[0:1, :], in_=macc[:, col:col + 1]), s_util)
        w1 = (s_util.sem, s_util.n)

        @block.sync
        def _(e, w1=w1):
            dwait(e, w1)
            dinc(e.dma_start(out=tinyB[0:1, :], in_=bnc_dram[0:1, :]), s_util)
        w2 = (s_util.sem, s_util.n)

        @block.vector
        def _(e, dst=dst, w2=w2):
            dwait(e, w2)
            e.reduce_max(tiny[0:1, dst:dst + 1], tinyB[0:1, :], axis=AXX).then_inc(VC.sem, 1)
        VC.n += 1
        return VC.n

    def allreduce_max(idx, c0, n, wait_vc):
        @block.vector
        def _(e, c0=c0, n=n, wv=wait_vc):
            e.wait_ge(VC.sem, wv)
            e.memset(tinyB[0:1, 0:8], NEG_PAD)
            e.drain()
            e.tensor_copy(tinyB[0:1, 0:n], tiny[0:1, c0:c0 + n])
            e.drain().then_inc(VC.sem, 1)
        VC.n += 1
        wv2 = VC.n

        @block.gpsimd
        def _(e, idx=idx, wv2=wv2):
            e.wait_ge(VC.sem, wv2)
            dinc(e.dma_start(out=cc_in[idx][0:1, :], in_=tinyB[0:1, 0:8]), g_ar)
        w1 = (g_ar.sem, g_ar.n)

        @block.gpsimd
        def _(e, idx=idx, w1=w1):
            dwait(e, w1)
            e.collective_compute(
                "AllReduce", ALU.max, replica_groups=[list(range(8))],
                ins=[cc_in[idx].ap()], outs=[cc_out[idx].ap()],
            ).then_inc(CCs.sem, 1)
        CCs.n += 1
        wc = CCs.n

        @block.gpsimd
        def _(e, idx=idx, c0=c0, n=n, wc=wc):
            e.wait_ge(CCs.sem, wc)
            dinc(e.dma_start(out=tiny[0:1, c0:c0 + n], in_=cc_out[idx][0:1, 0:n]), g_ar)
        return (g_ar.sem, g_ar.n)

    def bcast_scalars(c0, n, d0, wait_vc):
        @block.sync
        def _(e, c0=c0, n=n, wv=wait_vc):
            e.wait_ge(VC.sem, wv)
            dinc(e.dma_start(out=scal_dram[0:1, 0:n], in_=tiny[0:1, c0:c0 + n]), s_util)
        w1 = (s_util.sem, s_util.n)

        @block.sync
        def _(e, n=n, d0=d0, w1=w1):
            dwait(e, w1)
            for j in range(n):
                dinc(e.dma_start(out=scv_sb[:, d0 + j:d0 + j + 1],
                                 in_=scal_dram[0:1, j:j + 1].partition_broadcast(128)),
                     s_util)
        return (s_util.sem, s_util.n)

    s12 = ExitStack()
    V_all = s12.enter_context(nc.sbuf_tensor("V_all", [128, 32 * 8 * 130], BF16))

    s01 = ExitStack()
    wmod_sb = s01.enter_context(nc.sbuf_tensor("wmod_sb", [128, 8 * 512], F32))
    modrow = s01.enter_context(nc.sbuf_tensor("modrow", [1, 6 * D], F32))
    modtmp = s01.enter_context(nc.sbuf_tensor("modtmp", [1, 2 * D], F32))

    @block.sync
    def _(e):
        dinc(e.dma_start(out=dmask_sb[:, :], in_=dmask[:, :]), s_cons)
        dinc(e.dma_start(out=silu_sb[:, :], in_=cvec[:, :]), s_cons)
    sd_consts = (s_cons.sem, s_cons.n)

    @block.gpsimd
    def _(e):
        dinc(e.dma_start(out=eshift_sb[:, :], in_=eshift[:, :]), g_es)
        dinc(e.dma_start(out=ident_sb[:, :], in_=ident[:, :]), g_es)
    gd_eshift = (g_es.sem, g_es.n)

    @block.vector
    def _(e):
        e.memset(ones_bf[:, :], 1.0)
        e.memset(eps_sb[:, 0:1], EPS)
        e.memset(mgc_sb[:, 0:1], MAGIC)
        e.memset(nmg_sb[:, 0:1], -MAGIC)
        e.memset(ones_f[0:1, :], 1.0).then_inc(VC.sem, 1)
    VC.n += 1
    vc_ones = VC.n

    @block.scalar
    def _(e, w=sd_consts):
        dwait(e, w)
        e.activation(silu_sb[:, :], silu_sb[:, :], AF.Silu).then_inc(SA.sem, 1)
    SA.n += 1
    sa_silu = SA.n

    def finish_dummy(close_stacks=()):
        dump = os.environ.get("K_DUMP", "")
        with nc.sbuf_tensor("dummy_out", [128, 2048], F32) as dout, \
             nc.sbuf_tensor("dummy_bf", [128, 2048], BF16) as dbf:
            if dump:
                dsrc = {"x2": x2_dram, "h1": h1f_dram, "upf": upf_dram}[dump]
                for c2 in range(8):
                    @block.sync
                    def _(e, c2=c2, dump=dump, tok=(s_out[1].sem, s_out[1].n)):
                        dwait(e, tok)
                        srcap = (dsrc[c2 * 128:(c2 + 1) * 128, :] if dump == "upf"
                                 else dsrc[:, c2 * 2048:(c2 + 1) * 2048])
                        dinc(e.dma_start(out=dbf[:, :], in_=srcap), s_out[1])

                    @block.vector
                    def _(e, tok=(s_out[1].sem, s_out[1].n)):
                        dwait(e, tok)
                        e.tensor_copy(dout[:, :], dbf[:, :]).then_inc(VC.sem, 1)
                    VC.n += 1

                    @block.sync
                    def _(e, c2=c2, wv=VC.n):
                        e.wait_ge(VC.sem, wv)
                        dinc(e.dma_start(out=out[c2 * 128:(c2 + 1) * 128, :],
                                         in_=dout[:, :]), s_out[0])
                @block.sync
                def _(e, tok=(s_out[0].sem, s_out[0].n)):
                    dwait(e, tok)
            else:
                @block.vector
                def _(e):
                    e.memset(dout[:, :], 0.0).then_inc(VC.sem, 1)
                VC.n += 1
                wv = VC.n

                @block.sync
                def _(e, wv=wv):
                    e.wait_ge(VC.sem, wv)
                    for Dc in range(8):
                        dinc(e.dma_start(out=out[Dc * 128:(Dc + 1) * 128, :],
                                         in_=dout[:, :]), s_out[0])

                @block.sync
                def _(e, tok=(s_out[0].sem, s_out[0].n)):
                    dwait(e, tok)
        for stk in close_stacks:
            stk.close()
        es.close()
        return nc

    if _SUB < 1:
        return finish_dummy((s01, s12))
    mv_mm, mv_ev, mv_ld = [], [], []
    for n in range(12):
        @block.sync
        def _(e, n=n):
            if n >= 1:
                e.wait_ge(TM.sem, mv_mm[n - 1])
            for j in range(8):
                dinc(e.dma_start(out=wmod_sb[:, j * 512:(j + 1) * 512],
                                 in_=wmod[j * 128:(j + 1) * 128, n * 512:(n + 1) * 512]),
                     s_wmod)
        mv_ld.append((s_wmod.sem, s_wmod.n))

        @block.tensor
        def _(e, n=n, ld=mv_ld[n], silu_w=sa_silu):
            dwait(e, ld)
            e.wait_ge(SA.sem, silu_w)
            if n >= 1:
                e.wait_ge(VC.sem, mv_ev[n - 1])
            for j in range(8):
                ins = e.matmul(ring1[0:1, 0:512], silu_sb[:, j:j + 1],
                               wmod_sb[:, j * 512:(j + 1) * 512],
                               start=(j == 0), stop=(j == 7))
            ins.then_inc(TM.sem, 1)
        TM.n += 1
        mv_mm.append(TM.n)

        @block.vector
        def _(e, n=n, mm=mv_mm[n]):
            e.wait_ge(TM.sem, mm)
            e.tensor_copy(modrow[0:1, n * 512:(n + 1) * 512],
                          ring1[0:1, 0:512]).then_inc(VC.sem, 1)
        VC.n += 1
        mv_ev.append(VC.n)

    if _SUB < 2:
        return finish_dummy((s01, s12))
    @block.vector
    def _(e, w=mv_ev[-1]):
        e.wait_ge(VC.sem, w)
        e.tensor_single_scalar(modrow[0:1, 1024:2048], modrow[0:1, 1024:2048], 1.0, ALU.add)
        e.tensor_single_scalar(modrow[0:1, 4096:5120], modrow[0:1, 4096:5120], 1.0, ALU.add)
        for base in (2048, 5120):
            ga = modrow[0:1, base:base + 1024]
            sgn = modtmp[0:1, 0:1024]
            nga = modtmp[0:1, 1024:2048]
            e.tensor_scalar(sgn, ga, 0.0, None, ALU.is_ge)
            e.tensor_scalar(sgn, sgn, 2.0, -1.0, ALU.mult, ALU.add)
            e.tensor_single_scalar(nga, ga, -1.0, ALU.mult)
            e.tensor_tensor(ga, ga, nga, ALU.max)
            e.tensor_single_scalar(ga, ga, 0.25, ALU.max)
            ins = e.tensor_tensor(ga, ga, sgn, ALU.mult)
        ins.then_inc(VC.sem, 1)
    VC.n += 1
    vc_modpost = VC.n

    if _SUB < 3:
        return finish_dummy((s01, s12))
    @block.sync
    def _(e, w=vc_modpost):
        e.wait_ge(VC.sem, w)
        dinc(e.dma_start(out=mod_dram[0:1, :], in_=modrow[0:1, :]), s_util)
    sd_modout = (s_util.sem, s_util.n)

    mod4 = mod_dram.ap().rearrange("o (m c p) -> o m p c", m=6, c=8)

    @block.sync
    def _(e, w=sd_modout):
        dwait(e, w)
        for dst, src_ap in ((sa_sb[:, :], mod4[0:1, 0, :, :]),
                            (ca1_sb[:, :], mod4[0:1, 1, :, :]),
                            (ga_sb[:, :], mod4[0:1, 2, :, :]),
                            (gaup_sb[:, :], mod4[0:1, 2, 64:128, :]),
                            (sbv_sb[:, :], mod4[0:1, 3, :, :]),
                            (cb1_sb[:, :], mod4[0:1, 4, :, :]),
                            (gb_sb[:, :], mod4[0:1, 5, :, :])):
            dinc(e.dma_start(out=dst, in_=src_ap,
                             allow_slow_non_contiguous=True), s_util)
    sd_modvec = (s_util.sem, s_util.n)

    if _PHASES < 1:
        return finish_dummy((s01, s12))

    p1 = ExitStack()
    xbf = p1.enter_context(nc.sbuf_tensor("xbf", [128, 2 * 8 * 512], BF16))
    xsq = p1.enter_context(nc.sbuf_tensor("xsq", [128, 8 * 512], BF16))
    h1t = p1.enter_context(nc.sbuf_tensor("h1t", [128, 2 * 8 * 512], BF16))
    mrep = p1.enter_context(nc.sbuf_tensor("mrep", [128, 512], F32))
    rrep = p1.enter_context(nc.sbuf_tensor("rrep", [128, 512], F32))
    wslab = p1.enter_context(nc.sbuf_tensor("wslab", [128, 2 * 8 * 1024], BF16))
    qt_sb = p1.enter_context(nc.sbuf_tensor("qt_sb", [128, 2 * 512], BF16))

    @block.vector
    def _(e):
        v4 = V_all[:, :].rearrange("p (kc sl c) -> p kc sl c", kc=32, sl=16)
        e.memset(v4[:, :, :, 64:65], 1.0).then_inc(VC.sem, 1)
    VC.n += 1
    vc_vones = VC.n

    xcast_l, wsl_l, vev_all = [], [], []
    h1_l, vproj_l, h1sp_l, stats_l, rstd_l, xsq_l = [], [], [], [], [], []
    slab_done = []
    h1t_done = []
    qev_last = []

    for t in range(12):
        tb = t % 2
        full = t < 8

        @block.gpsimd
        def _(e, t=t, tb=tb, full=full):
            if t >= 2:
                e.wait_ge(VC.sem, h1_l[t - 2])
            for d in range(8):
                src = (xT[d * 128:(d + 1) * 128, t * 512:(t + 1) * 512] if full else
                       xoT[d * 128:(d + 1) * 128, (t - 8) * 512:(t - 7) * 512])
                dinc(e.dma_start(out=xbf[:, (tb * 8 + d) * 512:(tb * 8 + d + 1) * 512],
                                 in_=src), g_x[tb])
        xcast_l.append((g_x[tb].sem, g_x[tb].n))

        @block.gpsimd
        def _(e, t=t, tb=tb, full=full):
            if t >= 2:
                e.wait_ge(TM.sem, slab_done[t - 2])
            cols = (2048, 3072) if full else (0, 1024)
            for d in range(8):
                dinc(e.dma_start(out=wslab[:, (tb * 8 + d) * 1024:(tb * 8 + d + 1) * 1024],
                                 in_=wqkv[d * 128:(d + 1) * 128, cols[0]:cols[1]]),
                     g_w[tb])
        wsl_l.append((g_w[tb].sem, g_w[tb].n))

        @block.vector
        def _(e, t=t, tb=tb):
            dwait(e, xcast_l[t])
            if t >= 1:
                e.wait_ge(TM.sem, stats_l[t - 1])
            for d in range(8):
                xb = xbf[:, (tb * 8 + d) * 512:(tb * 8 + d + 1) * 512]
                ins = e.tensor_tensor(xsq[:, d * 512:(d + 1) * 512], xb, xb, ALU.mult)
            ins.then_inc(VC.sem, 1)
        VC.n += 1
        xsq_l.append(VC.n)

        @block.tensor
        def _(e, t=t, tb=tb, ws=xsq_l[t], oz=vc_ones):
            e.wait_ge(VC.sem, ws)
            e.wait_ge(VC.sem, oz)
            if t >= 1:
                e.wait_ge(VC.sem, rstd_l[t - 1])
            for d in range(8):
                xb = xbf[:, (tb * 8 + d) * 512:(tb * 8 + d + 1) * 512]
                e.matmul(pA[:, :], ones_bf[:, :], xb, start=(d == 0), stop=(d == 7))
            for d in range(8):
                ins = e.matmul(pB[:, :], ones_bf[:, :], xsq[:, d * 512:(d + 1) * 512],
                               start=(d == 0), stop=(d == 7))
            ins.then_inc(TM.sem, 1)
        TM.n += 1
        stats_l.append(TM.n)

        @block.vector
        def _(e, t=t, w=stats_l[t]):
            e.wait_ge(TM.sem, w)
            e.tensor_single_scalar(mrep[:, :], pA[:, :], 1.0 / 1024.0, ALU.mult)
            e.tensor_tensor(rrep[:, :], mrep[:, :], mrep[:, :], ALU.mult)
            e.scalar_tensor_tensor(rrep[:, :], pB[:, :], 1.0 / 1024.0, rrep[:, :],
                                   ALU.mult, ALU.subtract).then_inc(VC.sem, 1)
        VC.n += 1
        var_v = VC.n

        @block.scalar
        def _(e, w=var_v):
            e.wait_ge(VC.sem, w)
            e.activation(rrep[:, :], rrep[:, :], AF.Sqrt,
                         bias=eps_sb[:, 0:1]).then_inc(SA.sem, 1)
        SA.n += 1
        sq_v = SA.n

        @block.vector
        def _(e, w=sq_v):
            e.wait_ge(SA.sem, w)
            e.reciprocal(rrep[:, :], rrep[:, :]).then_inc(VC.sem, 1)
            VC.n += 1
            e.wait_ge(VC.sem, VC.n)
        rstd_l.append(VC.n)

        @block.vector
        def _(e, t=t, tb=tb, mv=sd_modvec):
            dwait(e, mv)
            if t >= 2:
                for kind, val in h1t_done[t - 2]:
                    if kind == "TM":
                        e.wait_ge(TM.sem, val)
                    elif kind == "SA":
                        e.wait_ge(SA.sem, val)
                    else:
                        dwait(e, val)
            for d in range(8):
                xb = xbf[:, (tb * 8 + d) * 512:(tb * 8 + d + 1) * 512]
                h1 = h1t[:, (tb * 8 + d) * 512:(tb * 8 + d + 1) * 512]
                e.tensor_tensor(h1, xb, mrep[:, :], ALU.subtract)
                e.tensor_tensor(h1, h1, rrep[:, :], ALU.mult)
                ins = e.tensor_scalar(h1, h1, ca1_sb[:, d:d + 1], sa_sb[:, d:d + 1],
                                      ALU.mult, ALU.add)
            ins.then_inc(VC.sem, 1)
        VC.n += 1
        h1_l.append(VC.n)

        if full:
            @block.sync
            def _(e, t=t, tb=tb, w=h1_l[t]):
                e.wait_ge(VC.sem, w)
                for d in range(8):
                    dinc(e.dma_start(
                        out=h1f_dram[:, (d * 8 + t) * 512:(d * 8 + t + 1) * 512],
                        in_=h1t[:, (tb * 8 + d) * 512:(tb * 8 + d + 1) * 512]),
                        s_h1sp[tb])
            h1sp_l.append((s_h1sp[tb].sem, s_h1sp[tb].n))
            spill_tok = h1sp_l[-1]

            for tt in range(4):
                ring = ring0 if tt % 2 == 0 else ring1
                gi = t * 4 + tt

                @block.tensor
                def _(e, t=t, tb=tb, tt=tt, ring=ring, gi=gi, wv=wsl_l[t], hv=h1_l[t]):
                    if tt == 0:
                        dwait(e, wv)
                        e.wait_ge(VC.sem, hv)
                    if gi >= 2:
                        e.wait_ge(VC.sem, vev_all[gi - 2])
                    for d in range(8):
                        h1 = h1t[:, (tb * 8 + d) * 512 + tt * 128:
                                 (tb * 8 + d) * 512 + (tt + 1) * 128]
                        w0 = wslab[:, (tb * 8 + d) * 1024:(tb * 8 + d) * 1024 + 512]
                        w1 = wslab[:, (tb * 8 + d) * 1024 + 512:
                                   (tb * 8 + d) * 1024 + 1024]
                        e.matmul(ring[:, 0:512], h1, w0, start=(d == 0), stop=(d == 7))
                        ins = e.matmul(ring[:, 512:1024], h1, w1, start=(d == 0),
                                       stop=(d == 7))
                    ins.then_inc(TM.sem, 1)
                TM.n += 1
                vmm = TM.n

                @block.vector
                def _(e, t=t, tt=tt, ring=ring, w=vmm):
                    e.wait_ge(TM.sem, w)
                    base = (t * 4 + tt) * 1040
                    dst0 = V_all[:, base:base + 520].rearrange(
                        "p (sl c) -> p sl c", c=65)[:, :, 0:64]
                    dst1 = V_all[:, base + 520:base + 1040].rearrange(
                        "p (sl c) -> p sl c", c=65)[:, :, 0:64]
                    src0 = ring[:, 0:512].rearrange("p (sl c) -> p sl c", c=64)
                    src1 = ring[:, 512:1024].rearrange("p (sl c) -> p sl c", c=64)
                    e.tensor_copy(dst0, src0)
                    e.tensor_copy(dst1, src1).then_inc(VC.sem, 1)
                VC.n += 1
                vev_all.append(VC.n)
            vproj_l.append(TM.n)
            slab_done.append(TM.n)
            h1t_done.append([("TM", TM.n), ("SD", spill_tok)])
        else:
            ot = t - 8
            qmm_sa2, qst_sd2 = [], []
            for p in range(NPAIR):
                ps = pC if p % 2 == 0 else pD

                @block.tensor
                def _(e, t=t, tb=tb, p=p, ps=ps, wq=wsl_l[t], hv=h1_l[t]):
                    if p == 0:
                        dwait(e, wq)
                        e.wait_ge(VC.sem, hv)
                    if p >= 2:
                        e.wait_ge(SA.sem, qmm_sa2[p - 2])
                    elif t >= 9 and p < 2:
                        e.wait_ge(SA.sem, qev_last[-1])
                    for d in range(8):
                        wqt = wslab[:, (tb * 8 + d) * 1024 + p * 128:
                                    (tb * 8 + d) * 1024 + (p + 1) * 128]
                        ins = e.matmul(ps[:, :], wqt,
                                       h1t[:, (tb * 8 + d) * 512:(tb * 8 + d + 1) * 512],
                                       start=(d == 0), stop=(d == 7))
                    ins.then_inc(TM.sem, 1)
                TM.n += 1
                qmm = TM.n

                @block.scalar
                def _(e, p=p, ps=ps, w=qmm):
                    e.wait_ge(TM.sem, w)
                    if p >= 2:
                        dwait(e, qst_sd2[p - 2])
                    e.copy(qt_sb[:, (p % 2) * 512:(p % 2 + 1) * 512],
                           ps[:, :]).then_inc(SA.sem, 1)
                SA.n += 1
                qmm_sa2.append(SA.n)

                @block.sync
                def _(e, ot=ot, p=p, w=qmm_sa2[p]):
                    e.wait_ge(SA.sem, w)
                    dinc(e.dma_start(
                        out=q_dram[:, p * OWN + ot * 512:p * OWN + (ot + 1) * 512],
                        in_=qt_sb[:, (p % 2) * 512:(p % 2 + 1) * 512]),
                        s_qsp[p % 2])
                qst_sd2.append((s_qsp[p % 2].sem, s_qsp[p % 2].n))
            qev_last.append(SA.n)
            slab_done.append(TM.n)
            h1t_done.append([("TM", TM.n)])

    p1.close()
    if _PHASES < 2:
        return finish_dummy((s01, s12))
    barrier()
    s01.close()

    p2 = ExitStack()
    K_p = p2.enter_context(nc.sbuf_tensor("K_p", [128, 2 * S], BF16))
    QA_sb = p2.enter_context(nc.sbuf_tensor("QA_sb", [64, 2 * OWN], BF16))
    QB_sb = p2.enter_context(nc.sbuf_tensor("QB_sb", [64, 2 * OWN], BF16))
    NKT_A = p2.enter_context(nc.sbuf_tensor("NKT_A", [64, S], BF16))
    NKT_B = p2.enter_context(nc.sbuf_tensor("NKT_B", [64, S], BF16))
    P_sb = p2.enter_context(nc.sbuf_tensor("P_sb", [128, 2 * 1024], BF16))
    h1s = p2.enter_context(nc.sbuf_tensor("h1s", [128, 2 * 8 * 512], BF16))
    wks = p2.enter_context(nc.sbuf_tensor("wks", [128, 2 * 8 * 128], BF16))
    ksq_sb = p2.enter_context(nc.sbuf_tensor("ksq_sb", [128, 512], BF16))
    recA = p2.enter_context(nc.sbuf_tensor("recA", [64, 512], F32))
    recB = p2.enter_context(nc.sbuf_tensor("recB", [64, 512], F32))
    u1s = p2.enter_context(nc.sbuf_tensor("u1s", [64, 512], BF16))
    u2s = p2.enter_context(nc.sbuf_tensor("u2s", [64, 512], BF16))
    x2t = p2.enter_context(nc.sbuf_tensor("x2t", [128, 2 * 512], BF16))
    xot = p2.enter_context(nc.sbuf_tensor("xot", [128, 2 * 512], F32))
    dq_in = p2.enter_context(nc.sbuf_tensor("dq_in", [128, 2 * 1024], F32))
    dq_t = p2.enter_context(nc.sbuf_tensor("dq_t", [128, 1024], F32))
    dq_o = p2.enter_context(nc.sbuf_tensor("dq_o", [128, 2 * 1024], BF16))
    dq_s = p2.enter_context(nc.sbuf_tensor("dq_s", [128, 32], F32))
    dq_r = p2.enter_context(nc.sbuf_tensor("dq_r", [128, 32], F32))

    dq_plan = []
    for wsrc, wdst, nrow, ncol in ((wup_n, wupq_dram, I3, D), (wgate_n, wgq_dram, I3, D),
                                   (wdn_n, wdnq_dram, D, I3)):
        for r0 in range(0, nrow, 128):
            for c0 in range(0, ncol, 1024):
                dq_plan.append((wsrc, wdst, r0, c0))
    dq_state = {"i": 0, "g": [], "st": [], "v": []}

    def emit_dequant(k):
        for _ in range(k):
            i = dq_state["i"]
            if i >= len(dq_plan):
                return
            dq_state["i"] += 1
            wsrc, wdst, r0, c0 = dq_plan[i]
            buf = i % 2

            @block.sync
            def _(e, wsrc=wsrc, r0=r0, c0=c0, buf=buf, i=i):
                if i >= 2:
                    e.wait_ge(GC.sem, dq_state["g"][i - 2][0])
                dinc(e.dma_start(out=dq_in[:, buf * 1024:(buf + 1) * 1024],
                                 in_=wsrc[r0:r0 + 128, c0:c0 + 1024]), s_dqld[buf])
            ld = (s_dqld[buf].sem, s_dqld[buf].n)

            @block.vector
            def _(e, buf=buf, ld=ld, i=i):
                dwait(e, ld)
                if i >= 1:
                    e.wait_ge(GC.sem, dq_state["g"][i - 1][1])
                w3 = dq_in[:, buf * 1024:(buf + 1) * 1024].rearrange(
                    "p (b k) -> p b k", k=32)
                e.reduce_max(dq_s[:, :].unsqueeze(2), w3, axis=AXX,
                             apply_absolute_value=True)
                e.tensor_scalar(dq_s[:, :], dq_s[:, :], 1.0 / 7.0, 1e-12,
                                ALU.mult, ALU.max).then_inc(VC.sem, 1)
                VC.n += 1
                e.wait_ge(VC.sem, VC.n)
                e.reciprocal(dq_r[:, :], dq_s[:, :]).then_inc(VC.sem, 1)
            VC.n += 1
            vv = VC.n
            dq_state["v"].append(vv)

            @block.gpsimd
            def _(e, buf=buf, vv=vv, i=i):
                e.wait_ge(VC.sem, vv)
                if i >= 2:
                    dwait(e, dq_state["st"][i - 2])
                w3 = dq_in[:, buf * 1024:(buf + 1) * 1024].rearrange(
                    "p (b k) -> p b k", k=32)
                r3 = dq_r[:, :].unsqueeze(2).broadcast_to([128, 32, 32])
                s3 = dq_s[:, :].unsqueeze(2).broadcast_to([128, 32, 32])
                t3 = dq_t[:, :].rearrange("p (b k) -> p b k", k=32)
                o3 = dq_o[:, buf * 1024:(buf + 1) * 1024].rearrange(
                    "p (b k) -> p b k", k=32)
                e.tensor_tensor(t3, w3, r3, ALU.mult).then_inc(GC.sem, 1)
                e.tensor_scalar(dq_t[:, :], dq_t[:, :], MAGIC, -MAGIC, ALU.add, ALU.add)
                e.tensor_tensor(o3, t3, s3, ALU.mult).then_inc(GC.sem, 1)
            GC.n += 2
            dq_state["g"].append((GC.n - 1, GC.n))

            @block.sync
            def _(e, wdst=wdst, r0=r0, c0=c0, buf=buf, g=GC.n):
                e.wait_ge(GC.sem, g)
                dinc(e.dma_start(out=wdst[r0:r0 + 128, c0:c0 + 1024],
                                 in_=dq_o[:, buf * 1024:(buf + 1) * 1024]), s_dqst[buf])
            dq_state["st"].append((s_dqst[buf].sem, s_dqst[buf].n))

    @block.sync
    def _(e, toks=[(s.sem, s.n) for s in (s_h1sp[0], s_h1sp[1], s_qsp[0], s_qsp[1])]):
        for tk in toks:
            dwait(e, tk)

    av_free_vc = VC.n
    h1s_free = []
    kseq = 0
    bias_done = [0] * NPAIR
    kev_sa = []
    last_exp = []
    epv_seq = []
    x2st_seq = []

    attn_tm_last = []
    last_kbred = []
    for p in range(NPAIR):
        pb = p % 2

        @block.gpsimd
        def _(e, p=p, pb=pb):
            if p >= 2:
                e.wait_ge(TM.sem, kmm_last[p - 2])
            for d in range(8):
                dinc(e.dma_start(out=wks[:, (pb * 8 + d) * 128:(pb * 8 + d + 1) * 128],
                                 in_=wqkv[d * 128:(d + 1) * 128,
                                          1024 + p * 128:1024 + (p + 1) * 128]),
                     g_wk[pb])
        wk_d = (g_wk[pb].sem, g_wk[pb].n)

        @block.sync
        def _(e, p=p, pb=pb):
            if p >= 2:
                e.wait_ge(TM.sem, attn_tm_last[p - 2])
            dinc(e.dma_start(out=QA_sb[:, pb * OWN:(pb + 1) * OWN],
                             in_=q_dram[0:64, p * OWN:(p + 1) * OWN]), s_qpld[pb])
            dinc(e.dma_start(out=QB_sb[:, pb * OWN:(pb + 1) * OWN],
                             in_=q_dram[64:128, p * OWN:(p + 1) * OWN]), s_qpld[pb])
        qld_sd = (s_qpld[pb].sem, s_qpld[pb].n)

        kmm_l = []
        kev_t = []
        for t in range(8):
            tb2 = kseq % 2
            kseq += 1

            @block.sync
            def _(e, t=t, tb2=tb2, ks=kseq - 1):
                if ks >= 2:
                    e.wait_ge(TM.sem, h1s_free[ks - 2])
                for d in range(8):
                    dinc(e.dma_start(
                        out=h1s[:, (tb2 * 8 + d) * 512:(tb2 * 8 + d + 1) * 512],
                        in_=h1f_dram[:, (d * 8 + t) * 512:(d * 8 + t + 1) * 512]),
                        s_h1ld[tb2])
            hld = (s_h1ld[tb2].sem, s_h1ld[tb2].n)

            @block.tensor
            def _(e, p=p, t=t, tb2=tb2, pb=pb, hld=hld, wk=wk_d, af=av_free_vc,
                  kprev=(None if t == 0 else None)):
                dwait(e, hld)
                if t == 0:
                    dwait(e, wk)
                    e.wait_ge(VC.sem, af)
                else:
                    e.wait_ge(SA.sem, kev_t[t - 1])
                for d in range(8):
                    ins = e.matmul(
                        pC[:, :],
                        wks[:, (pb * 8 + d) * 128:(pb * 8 + d + 1) * 128],
                        h1s[:, (tb2 * 8 + d) * 512:(tb2 * 8 + d + 1) * 512],
                        start=(d == 0), stop=(d == 7))
                ins.then_inc(TM.sem, 1)
            TM.n += 1
            kmm_l.append(TM.n)
            h1s_free.append(TM.n)

            @block.scalar
            def _(e, p=p, t=t, pb=pb, w=kmm_l[t]):
                e.wait_ge(TM.sem, w)
                e.copy(K_p[:, pb * S + t * 512:pb * S + (t + 1) * 512],
                       pC[:, :]).then_inc(SA.sem, 1)
            SA.n += 1
            kev_t.append(SA.n)
        kev_sa.append(SA.n)
        if p == 0:
            kmm_last = []
        kmm_last.append(kmm_l[-1])

        ntr_mm, ntr_ev = [], []
        for r in range(64):
            psl = pC if r % 2 == 0 else pD

            @block.tensor
            def _(e, p=p, pb=pb, r=r, psl=psl, ke=kev_sa[p], ge=gd_eshift,
                  kbp=(last_kbred[-1] if last_kbred else 0)):
                if r == 0:
                    e.wait_ge(SA.sem, ke)
                    dwait(e, ge)
                if r <= 1 and kbp:
                    e.wait_ge(VC.sem, kbp)
                if r >= 2:
                    e.wait_ge(VC.sem, ntr_ev[r - 2])
                ps_bf = psl[:, :].bitcast(BF16)
                e.transpose(ps_bf[0:64, 0:128],
                            K_p[:, pb * S + r * 64:pb * S + (r + 1) * 64],
                            ident_sb[:, :]).then_inc(TM.sem, 1)
            TM.n += 1
            ntr_mm.append(TM.n)

            @block.vector
            def _(e, r=r, psl=psl, w=ntr_mm[r]):
                e.wait_ge(TM.sem, w)
                dstA = NKT_A[:, :].rearrange("p (g r) -> p g r", r=64)[:, :, r]
                dstB = NKT_B[:, :].rearrange("p (g r) -> p g r", r=64)[:, :, r]
                ps_bf = psl[:, :].bitcast(BF16)
                e.tensor_copy(dstA, ps_bf[0:64, 0:64])
                e.tensor_copy(dstB, ps_bf[0:64, 64:128]).then_inc(VC.sem, 1)
            VC.n += 1
            ntr_ev.append(VC.n)
        nkt_done = VC.n

        qb_mm, qb_red = [], []
        for sl in range(8):
            qsrc = QA_sb if sl < 4 else QB_sb
            so = (sl % 4) * 512

            @block.vector
            def _(e, pb=pb, sl=sl, qsrc=qsrc, so=so, nd=nkt_done, ql=qld_sd):
                if sl == 0:
                    e.wait_ge(VC.sem, nd)
                    dwait(e, ql)
                    e.memset(macc[:, 0:1], 0.0)
                if sl >= 1:
                    e.wait_ge(TM.sem, qb_mm[sl - 1])
                qq = qsrc[:, pb * OWN + so:pb * OWN + so + 512]
                e.tensor_tensor(ksq_sb[0:64, :], qq, qq, ALU.mult).then_inc(VC.sem, 1)
            VC.n += 1
            qv = VC.n

            @block.tensor
            def _(e, sl=sl, qv=qv, af=av_free_vc):
                e.wait_ge(VC.sem, qv)
                if sl == 0:
                    e.wait_ge(VC.sem, af)
                else:
                    e.wait_ge(VC.sem, qb_red[sl - 1])
                e.matmul(pD[:, :], ones_bf[0:64, :], ksq_sb[0:64, :], start=True,
                         stop=True).then_inc(TM.sem, 1)
            TM.n += 1
            qb_mm.append(TM.n)

            @block.vector
            def _(e, sl=sl, w=qb_mm[sl]):
                e.wait_ge(TM.sem, w)
                e.reduce_max(macc[:, 1:2], pD[:, :], axis=AXX)
                e.tensor_tensor(macc[:, 0:1], macc[:, 0:1], macc[:, 1:2],
                                ALU.max).then_inc(VC.sem, 1)
            VC.n += 1
            qb_red.append(VC.n)

        kb_mm, kb_red = [], []
        for sl in range(16):
            ksrc = NKT_A if sl < 8 else NKT_B
            so = (sl % 8) * 512

            @block.vector
            def _(e, sl=sl, ksrc=ksrc, so=so, qw=qb_red[-1]):
                if sl == 0:
                    e.wait_ge(VC.sem, qw)
                    e.memset(macc[:, 2:3], 0.0)
                if sl >= 1:
                    e.wait_ge(TM.sem, kb_mm[sl - 1])
                kk = ksrc[:, so:so + 512]
                e.tensor_tensor(ksq_sb[0:64, :], kk, kk, ALU.mult).then_inc(VC.sem, 1)
            VC.n += 1
            kv = VC.n

            @block.tensor
            def _(e, sl=sl, kv=kv):
                e.wait_ge(VC.sem, kv)
                if sl >= 1:
                    e.wait_ge(VC.sem, kb_red[sl - 1])
                e.matmul(pD[:, :], ones_bf[0:64, :], ksq_sb[0:64, :], start=True,
                         stop=True).then_inc(TM.sem, 1)
            TM.n += 1
            kb_mm.append(TM.n)

            @block.vector
            def _(e, sl=sl, w=kb_mm[sl]):
                e.wait_ge(TM.sem, w)
                e.reduce_max(macc[:, 3:4], pD[:, :], axis=AXX)
                e.tensor_tensor(macc[:, 2:3], macc[:, 2:3], macc[:, 3:4],
                                ALU.max).then_inc(VC.sem, 1)
            VC.n += 1
            kb_red.append(VC.n)

        last_kbred.append(kb_red[-1])

        @block.vector
        def _(e, p=p, w=kb_red[-1]):
            e.wait_ge(VC.sem, w)
            e.tensor_copy(tiny[0:1, 0:1], macc[0:1, 0:1])
            e.tensor_copy(tiny[0:1, 1:2], macc[0:1, 2:3])
            e.drain()
            e.tensor_tensor(tiny[0:1, 2:3], tiny[0:1, 0:1], tiny[0:1, 1:2],
                            ALU.mult)
            if p == 0:
                e.drain()
                e.tensor_copy(dbg_sb[0:1, 0:3], tiny[0:1, 0:3])
            e.drain().then_inc(VC.sem, 1)
        VC.n += 1
        bv = VC.n

        @block.scalar
        def _(e, bv=bv):
            e.wait_ge(VC.sem, bv)
            e.activation(tiny[0:1, 2:3], tiny[0:1, 2:3], AF.Sqrt).then_inc(SA.sem, 1)
        SA.n += 1
        bsq = SA.n

        @block.vector
        def _(e, p=p, bsq=bsq):
            e.wait_ge(SA.sem, bsq)
            if p == 0:
                e.tensor_copy(dbg_sb[0:1, 3:4], tiny[0:1, 2:3])
                e.drain()
            e.tensor_single_scalar(tiny[0:1, 2:3], tiny[0:1, 2:3], -0.125,
                                   ALU.mult)
            if p == 0:
                e.drain()
                e.tensor_copy(dbg_sb[0:1, 4:5], tiny[0:1, 2:3])
            e.drain().then_inc(VC.sem, 1)
        VC.n += 1
        bneg = VC.n

        @block.sync
        def _(e, bneg=bneg, p=p, pb=pb):
            e.wait_ge(VC.sem, bneg)
            if p >= 2:
                e.wait_ge(SA.sem, last_exp[p - 2])
            dinc(e.dma_start(out=scal_dram[0:1, 15:16], in_=tiny[0:1, 2:3]), s_bias[pb])
        b1 = (s_bias[pb].sem, s_bias[pb].n)

        @block.sync
        def _(e, b1=b1, p=p, pb=pb):
            dwait(e, b1)
            dinc(e.dma_start(out=bias_sb[:, pb:pb + 1],
                             in_=scal_dram[0:1, 15:16].partition_broadcast(128)),
                 s_bias[pb])
        bias_done[p] = (s_bias[pb].sem, s_bias[pb].n)

        for u in range(4):
            nch = 8 * (u + 1)
            qkt_d, mask_d, exp_d, av_d = {}, {}, {}, {}

            iseq = p * 4 + u

            @block.sync
            def _(e, p=p, u=u, iseq=iseq):
                if iseq >= 2:
                    e.wait_ge(VC.sem, epv_seq[iseq - 2])
                dinc(e.dma_start(out=xot[:, (u % 2) * 512:(u % 2 + 1) * 512],
                                 in_=xoT[p * 128:(p + 1) * 128, u * 512:(u + 1) * 512]),
                     s_xot[u % 2])
            xot_ld = (s_xot[u % 2].sem, s_xot[u % 2].n)

            for c in range(nch):
                ring = ring0 if c % 2 == 0 else ring1
                diag = c >= 8 * u
                qoff = 64 * (c - 8 * u) if diag else 0

                @block.tensor
                def _(e, p=p, pb=pb, u=u, c=c, ring=ring, qoff=qoff, af=av_free_vc):
                    if c >= 2:
                        e.wait_ge(SA.sem, exp_d[c - 2])
                    if c == 0:
                        e.wait_ge(VC.sem, af)
                    kA = NKT_A[:, c * 128:(c + 1) * 128]
                    kB = NKT_B[:, c * 128:(c + 1) * 128]
                    qA = QA_sb[:, pb * OWN + u * 512 + qoff:pb * OWN + (u + 1) * 512]
                    qB = QB_sb[:, pb * OWN + u * 512 + qoff:pb * OWN + (u + 1) * 512]
                    e.matmul(ring[:, qoff:512], kA, qA, start=True, stop=True)
                    e.matmul(ring[:, 512 + qoff:1024], kB, qB, start=True,
                             stop=True).then_inc(TM.sem, 1)
                TM.n += 1
                qkt_d[c] = TM.n

                if diag:
                    @block.vector
                    def _(e, c=c, ring=ring, qoff=qoff, w=qkt_d[c]):
                        e.wait_ge(TM.sem, w)
                        e.tensor_tensor(ring[:, qoff:qoff + 64], ring[:, qoff:qoff + 64],
                                        dmask_sb[:, :], ALU.add)
                        e.tensor_tensor(ring[:, 512 + qoff:512 + qoff + 64],
                                        ring[:, 512 + qoff:512 + qoff + 64],
                                        dmask_sb[:, :], ALU.add).then_inc(VC.sem, 1)
                    VC.n += 1
                    mask_d[c] = VC.n

                @block.scalar
                def _(e, p=p, pb=pb, c=c, ring=ring, diag=diag,
                      wq=qkt_d[c], wm=mask_d.get(c), wb=bias_done[p]):
                    if diag:
                        e.wait_ge(VC.sem, wm)
                    else:
                        e.wait_ge(TM.sem, wq)
                    if c == 0:
                        dwait(e, wb)
                    if c >= 2:
                        e.wait_ge(TM.sem, av_d[c - 2])
                    e.activation(P_sb[:, (c % 2) * 1024:(c % 2 + 1) * 1024], ring[:, :],
                                 AF.Exp, bias=bias_sb[:, pb:pb + 1],
                                 scale=0.125).then_inc(SA.sem, 1)
                SA.n += 1
                exp_d[c] = SA.n

                @block.tensor
                def _(e, p=p, c=c, qoff=qoff, nch=nch, w=exp_d[c]):
                    e.wait_ge(SA.sem, w)
                    base = c * 1040 + p * 130
                    lA = V_all[:, base:base + 65]
                    lB = V_all[:, base + 65:base + 130]
                    pa = P_sb[:, (c % 2) * 1024 + qoff:(c % 2) * 1024 + 512]
                    pbb = P_sb[:, (c % 2) * 1024 + 512 + qoff:(c % 2) * 1024 + 1024]
                    e.matmul(pA[0:65, qoff:512], lA, pa, start=(c == 0),
                             stop=(c == nch - 1))
                    e.matmul(pB[0:65, qoff:512], lB, pbb, start=(c == 0),
                             stop=(c == nch - 1)).then_inc(TM.sem, 1)
                TM.n += 1
                av_d[c] = TM.n

            @block.vector
            def _(e, p=p, u=u, w=av_d[nch - 1]):
                e.wait_ge(TM.sem, w)
                e.reciprocal(recA[0:1, :], pA[64:65, 0:512]).then_inc(VC.sem, 1)
                e.reciprocal(recB[0:1, :], pB[64:65, 0:512]).then_inc(VC.sem, 1)
                if p == 0 and u == 0:
                    e.drain()
                    e.tensor_copy(dbg_sb[0:1, 8:16], recA[0:1, 0:8])
                    e.tensor_copy(dbg_sb[0:1, 16:24], recB[0:1, 0:8])
            VC.n += 2
            rec_v = VC.n

            @block.tensor
            def _(e, rv=rec_v):
                e.wait_ge(VC.sem, rv)
                e.matmul(pC[0:64, :], ones_f[0:1, :], recA[0:1, :], start=True, stop=True)
                e.matmul(pD[0:64, :], ones_f[0:1, :], recB[0:1, :], start=True,
                         stop=True).then_inc(TM.sem, 1)
            TM.n += 1
            bc_t = TM.n

            @block.vector
            def _(e, p=p, u=u, bt=bc_t, xl=xot_ld, iseq=iseq):
                e.wait_ge(TM.sem, bt)
                dwait(e, xl)
                if iseq >= 2:
                    dwait(e, x2st_seq[iseq - 2])
                ub = u % 2
                e.tensor_copy(recA[:, :], pC[0:64, :])
                e.tensor_copy(recB[:, :], pD[0:64, :])
                if p == 0 and u == 0:
                    e.tensor_copy(dbg_sb[0:1, 24:32], P_sb[0:1, 1024:1032])
                    e.tensor_copy(dbg_sb[0:1, 32:40], pA[0:1, 0:8])
                    e.tensor_copy(dbg_sb[0:1, 40:48], QA_sb[0:1, 0:8])
                    e.tensor_copy(dbg_sb[0:1, 48:56], NKT_A[0:1, 0:8])
                e.scalar_tensor_tensor(u1s[:, :], pA[0:64, 0:512], ga_sb[0:64, p:p + 1],
                                       recA[:, :], ALU.mult, ALU.mult)
                e.tensor_tensor(x2t[0:64, ub * 512:(ub + 1) * 512], u1s[:, :],
                                xot[0:64, ub * 512:(ub + 1) * 512], ALU.add)
                e.scalar_tensor_tensor(u2s[:, :], pB[0:64, 0:512], gaup_sb[:, p:p + 1],
                                       recB[:, :], ALU.mult, ALU.mult).then_inc(VC.sem, 1)
            VC.n += 1
            ep_v = VC.n

            @block.tensor
            def _(e, ev=ep_v, ge=gd_eshift):
                e.wait_ge(VC.sem, ev)
                dwait(e, ge)
                e.matmul(ring0[:, 0:512], eshift_sb[:, :], u2s[:, :], start=True,
                         stop=True).then_inc(TM.sem, 1)
            TM.n += 1
            rel_t = TM.n

            @block.vector
            def _(e, p=p, u=u, rt=rel_t):
                e.wait_ge(TM.sem, rt)
                ub = u % 2
                e.tensor_tensor(x2t[64:128, ub * 512:(ub + 1) * 512], ring0[64:128, 0:512],
                                xot[64:128, ub * 512:(ub + 1) * 512],
                                ALU.add).then_inc(VC.sem, 1)
            VC.n += 1
            x2_v = VC.n
            av_free_vc = x2_v

            epv_seq.append(x2_v)

            @block.sync
            def _(e, p=p, u=u, xv=x2_v):
                e.wait_ge(VC.sem, xv)
                ub = u % 2
                dinc(e.dma_start(
                    out=x2_dram[:, p * OWN + u * 512:p * OWN + (u + 1) * 512],
                    in_=x2t[:, ub * 512:(ub + 1) * 512]), s_x2st[u % 2])
            x2st_seq.append((s_x2st[u % 2].sem, s_x2st[u % 2].n))

            emit_dequant(3)
        attn_tm_last.append(TM.n)
        last_exp.append(SA.n)
        emit_dequant(2)

    emit_dequant(len(dq_plan))
    dq_last_st = [(s.sem, s.n) for s in (s_dqst[0], s_dqst[1])]
    assert dq_state["i"] == len(dq_plan)

    p2.close()
    s12.close()
    if _PHASES < 3:
        return finish_dummy((s01,))
    barrier()

    s46 = ExitStack()
    gate_sb = s46.enter_context(nc.sbuf_tensor("gate_sb", [128, NCC * OWN], BF16))

    s34 = ExitStack()
    xq_sb = s34.enter_context(nc.sbuf_tensor("xq_sb", [128, 8 * OWN], BF16))

    p3 = ExitStack()
    h2_sb = p3.enter_context(nc.sbuf_tensor("h2_sb", [128, 8 * OWN], BF16))
    xqt_f = p3.enter_context(nc.sbuf_tensor("xqt_f", [128, OWN], F32))
    x2b = p3.enter_context(nc.sbuf_tensor("x2b", [128, 2 * 8 * 512], BF16))
    x2q = p3.enter_context(nc.sbuf_tensor("x2q", [128, 8 * 512], BF16))
    mrep2 = p3.enter_context(nc.sbuf_tensor("mrep2", [128, 512], F32))
    rrep2 = p3.enter_context(nc.sbuf_tensor("rrep2", [128, 512], F32))

    @block.sync
    def _(e, toks=[(s.sem, s.n) for s in (s_x2st[0], s_x2st[1])]):
        for tk in toks:
            dwait(e, tk)

    x2ld_l, st2_l, rst2_l, h2_l, x2q_l = [], [], [], [], []
    for t in range(4):
        tb = t % 2

        @block.sync
        def _(e, t=t, tb=tb):
            if t >= 2:
                e.wait_ge(VC.sem, h2_l[t - 2])
            for d in range(8):
                dinc(e.dma_start(
                    out=x2b[:, (tb * 8 + d) * 512:(tb * 8 + d + 1) * 512],
                    in_=x2_dram[:, d * OWN + t * 512:d * OWN + (t + 1) * 512]),
                    s_x2ld[tb])
        x2ld_l.append((s_x2ld[tb].sem, s_x2ld[tb].n))

        @block.vector
        def _(e, t=t, tb=tb, w=x2ld_l[t]):
            dwait(e, w)
            if t >= 1:
                e.wait_ge(TM.sem, st2_l[t - 1])
            for d in range(8):
                xb = x2b[:, (tb * 8 + d) * 512:(tb * 8 + d + 1) * 512]
                ins = e.tensor_tensor(x2q[:, d * 512:(d + 1) * 512], xb, xb, ALU.mult)
            ins.then_inc(VC.sem, 1)
        VC.n += 1
        x2q_l.append(VC.n)

        @block.tensor
        def _(e, t=t, tb=tb, w=x2q_l[t]):
            e.wait_ge(VC.sem, w)
            if t >= 1:
                e.wait_ge(VC.sem, rst2_l[t - 1])
            for d in range(8):
                xb = x2b[:, (tb * 8 + d) * 512:(tb * 8 + d + 1) * 512]
                e.matmul(pA[:, :], ones_bf[:, :], xb, start=(d == 0), stop=(d == 7))
            for d in range(8):
                ins = e.matmul(pB[:, :], ones_bf[:, :], x2q[:, d * 512:(d + 1) * 512],
                               start=(d == 0), stop=(d == 7))
            ins.then_inc(TM.sem, 1)
        TM.n += 1
        st2_l.append(TM.n)

        @block.vector
        def _(e, t=t, w=st2_l[t]):
            e.wait_ge(TM.sem, w)
            e.tensor_single_scalar(mrep2[:, :], pA[:, :], 1.0 / 1024.0, ALU.mult)
            e.tensor_tensor(rrep2[:, :], mrep2[:, :], mrep2[:, :], ALU.mult)
            e.scalar_tensor_tensor(rrep2[:, :], pB[:, :], 1.0 / 1024.0, rrep2[:, :],
                                   ALU.mult, ALU.subtract).then_inc(VC.sem, 1)
        VC.n += 1
        v2 = VC.n

        @block.scalar
        def _(e, w=v2):
            e.wait_ge(VC.sem, w)
            e.activation(rrep2[:, :], rrep2[:, :], AF.Sqrt, bias=eps_sb[:, 0:1]).then_inc(SA.sem, 1)
        SA.n += 1
        s2 = SA.n

        @block.vector
        def _(e, w=s2):
            e.wait_ge(SA.sem, w)
            e.reciprocal(rrep2[:, :], rrep2[:, :]).then_inc(VC.sem, 1)
            VC.n += 1
            e.wait_ge(VC.sem, VC.n)
        rst2_l.append(VC.n)

        @block.vector
        def _(e, t=t, tb=tb):
            if t == 0:
                e.memset(macc[:, 4:6], 0.0)
            for d in range(8):
                xb = x2b[:, (tb * 8 + d) * 512:(tb * 8 + d + 1) * 512]
                h2 = h2_sb[:, d * OWN + t * 512:d * OWN + (t + 1) * 512]
                e.tensor_tensor(h2, xb, mrep2[:, :], ALU.subtract)
                e.tensor_tensor(h2, h2, rrep2[:, :], ALU.mult)
                e.tensor_scalar(h2, h2, cb1_sb[:, d:d + 1], sbv_sb[:, d:d + 1],
                                ALU.mult, ALU.add)
                e.reduce_max(macc[:, 5:6], h2, axis=AXX, apply_absolute_value=True)
                ins = e.tensor_tensor(macc[:, 4:5], macc[:, 4:5], macc[:, 5:6], ALU.max)
            ins.then_inc(VC.sem, 1)
        VC.n += 1
        h2_l.append(VC.n)

    pm0 = part_max(4, 4, h2_l[-1])
    ar1 = allreduce_max(0, 4, 1, pm0)

    @block.vector
    def _(e, w=ar1):
        dwait(e, w)
        e.tensor_scalar(tiny[0:1, 5:6], tiny[0:1, 4:5], 1.0 / QMAX, 1e-12,
                        ALU.mult, ALU.max)
        recip1(e, tiny[0:1, 6:7], tiny[0:1, 5:6]).then_inc(VC.sem, 1)
    VC.n += 1
    s0_v = VC.n
    s0_sd = bcast_scalars(5, 2, 0, s0_v)

    @block.vector
    def _(e, w=s0_sd):
        dwait(e, w)
        for d in range(8):
            h2 = h2_sb[:, d * OWN:(d + 1) * OWN]
            xq = xq_sb[:, d * OWN:(d + 1) * OWN]
            e.tensor_scalar(xqt_f[:, :], h2, scv_sb[:, 1:2], MAGIC, ALU.mult, ALU.add)
            ins = e.tensor_scalar(xq, xqt_f[:, :], -MAGIC, scv_sb[:, 0:1],
                                  ALU.add, ALU.mult)
        ins.then_inc(VC.sem, 1)
    VC.n += 1

    p3.close()
    if _PHASES < 4:
        return finish_dummy((s34, s46))
    barrier()

    p4 = ExitStack()
    wtile = p4.enter_context(nc.sbuf_tensor("wtile", [128, 2 * 8 * 128], BF16))
    gtile = p4.enter_context(nc.sbuf_tensor("gtile", [128, 2 * 8 * 128], BF16))
    upt = p4.enter_context(nc.sbuf_tensor("upt", [128, 2 * OWN], BF16))

    @block.vector
    def _(e):
        e.memset(macc[:, 0:4], NEG_PAD)
        e.memset(macc[:, 6:8], NEG_PAD).then_inc(VC.sem, 1)
    VC.n += 1
    macc_z = VC.n

    upmm_l, gmm_l, upev_l, gev_l, upst_l, upmax_l = [], [], [], [], [], []
    ups = (pA[:, :], pB[:, :], pC[:, :], pD[:, :])
    gps = (ring0[:, 0:512], ring0[:, 512:1024], ring1[:, 0:512], ring1[:, 512:1024])

    for cc in range(NCC):
        b = cc % 2

        @block.scalar
        def _(e, cc=cc, b=b, dqw=dq_last_st):
            if cc == 0:
                for tk in dqw:
                    dwait(e, tk)
            if cc >= 2:
                e.wait_ge(TM.sem, upmm_l[cc - 2])
                e.wait_ge(TM.sem, gmm_l[cc - 2])
            for d in range(8):
                dinc(e.dma_start(
                    out=wtile[:, (b * 8 + d) * 128:(b * 8 + d + 1) * 128],
                    in_=wupq_dram[cc * 128:(cc + 1) * 128, d * 128:(d + 1) * 128],
                    transpose=True), c_w[b])
                dinc(e.dma_start(
                    out=gtile[:, (b * 8 + d) * 128:(b * 8 + d + 1) * 128],
                    in_=wgq_dram[cc * 128:(cc + 1) * 128, d * 128:(d + 1) * 128],
                    transpose=True), c_w[b])
        wt_d = (c_w[b].sem, c_w[b].n)

        @block.tensor
        def _(e, cc=cc, b=b, wd=wt_d):
            dwait(e, wd)
            if cc >= 1:
                e.wait_ge(SA.sem, upev_l[cc - 1])
            for d in range(8):
                wt = wtile[:, (b * 8 + d) * 128:(b * 8 + d + 1) * 128]
                for tt in range(4):
                    ins = e.matmul(
                        ups[tt], wt,
                        xq_sb[:, d * OWN + tt * 512:d * OWN + (tt + 1) * 512],
                        start=(d == 0), stop=(d == 7))
            ins.then_inc(TM.sem, 1)
        TM.n += 1
        upmm_l.append(TM.n)

        @block.scalar
        def _(e, cc=cc, b=b, w=upmm_l[cc]):
            e.wait_ge(TM.sem, w)
            if cc >= 2:
                dwait(e, upst_l[cc - 2])
                e.wait_ge(VC.sem, upmax_l[cc - 2])
            for tt in range(4):
                ins = e.copy(upt[:, b * OWN + tt * 512:b * OWN + (tt + 1) * 512],
                             ups[tt])
            ins.then_inc(SA.sem, 1)
        SA.n += 1
        upev_l.append(SA.n)

        @block.sync
        def _(e, cc=cc, b=b, w=upev_l[cc]):
            e.wait_ge(SA.sem, w)
            dinc(e.dma_start(out=upf_dram[cc * 128:(cc + 1) * 128, :],
                             in_=upt[:, b * OWN:(b + 1) * OWN]), s_upst[b])
        upst_l.append((s_upst[b].sem, s_upst[b].n))

        @block.vector
        def _(e, cc=cc, b=b, w=upev_l[cc], mz=macc_z):
            e.wait_ge(SA.sem, w)
            if cc == 0:
                e.wait_ge(VC.sem, mz)
            e.reduce_max(macc[:, 1:2], upt[:, b * OWN:(b + 1) * OWN], axis=AXX,
                         apply_absolute_value=True)
            e.tensor_tensor(macc[:, 0:1], macc[:, 0:1], macc[:, 1:2],
                            ALU.max).then_inc(VC.sem, 1)
        VC.n += 1
        upmax_l.append(VC.n)

        @block.tensor
        def _(e, cc=cc, b=b):
            if cc >= 1:
                e.wait_ge(VC.sem, gev_l[cc - 1])
            for d in range(8):
                gt = gtile[:, (b * 8 + d) * 128:(b * 8 + d + 1) * 128]
                for tt in range(4):
                    ins = e.matmul(
                        gps[tt], gt,
                        xq_sb[:, d * OWN + tt * 512:d * OWN + (tt + 1) * 512],
                        start=(d == 0), stop=(d == 7))
            ins.then_inc(TM.sem, 1)
        TM.n += 1
        gmm_l.append(TM.n)

        @block.vector
        def _(e, cc=cc, w=gmm_l[cc]):
            e.wait_ge(TM.sem, w)
            for tt in range(4):
                e.tensor_copy(gate_sb[:, cc * OWN + tt * 512:cc * OWN + (tt + 1) * 512],
                              gps[tt])
            gg = gate_sb[:, cc * OWN:(cc + 1) * OWN]
            e.reduce_max(macc[:, 3:4], gg, axis=AXX, apply_absolute_value=True)
            e.tensor_tensor(macc[:, 2:3], macc[:, 2:3], macc[:, 3:4], ALU.max)
            e.reduce_max(macc[:, 7:8], gg, axis=AXX)
            e.tensor_tensor(macc[:, 6:7], macc[:, 6:7], macc[:, 7:8],
                            ALU.max).then_inc(VC.sem, 1)
        VC.n += 1
        gev_l.append(VC.n)

    pm1 = part_max(0, 8, gev_l[-1])
    pm2 = part_max(2, 9, pm1)
    pm3 = part_max(6, 10, pm2)
    ar2 = allreduce_max(1, 8, 3, pm3)

    @block.vector
    def _(e, w=ar2):
        dwait(e, w)
        e.tensor_scalar(tiny[0:1, 12:13], tiny[0:1, 9:10], 1.0 / QMAX, 1e-12,
                        ALU.mult, ALU.max)
        recip1(e, tiny[0:1, 11:12], tiny[0:1, 12:13])
        e.tensor_scalar(tiny[0:1, 3:4], tiny[0:1, 8:9], 1.0 / QMAX, 1e-12,
                        ALU.mult, ALU.max)
        recip1(e, tiny[0:1, 14:15], tiny[0:1, 3:4])
        e.tensor_tensor(tiny[0:1, 0:1], tiny[0:1, 10:11], tiny[0:1, 11:12], ALU.mult)
        e.drain()
        e.tensor_scalar(tiny[0:1, 0:1], tiny[0:1, 0:1], MAGIC, -MAGIC,
                        ALU.add, ALU.add).then_inc(VC.sem, 1)
    VC.n += 1
    m3a = VC.n

    @block.scalar
    def _(e, w=m3a):
        e.wait_ge(VC.sem, w)
        e.activation(tiny[0:1, 1:2], tiny[0:1, 0:1], AF.Sigmoid,
                     scale=tiny[0:1, 12:13]).then_inc(SA.sem, 1)
    SA.n += 1
    m3b = SA.n

    @block.vector
    def _(e, w=m3b):
        e.wait_ge(SA.sem, w)
        e.tensor_scalar(tiny[0:1, 1:2], tiny[0:1, 1:2], 65536.0, MAGIC, ALU.mult, ALU.add)
        e.drain()
        e.tensor_scalar(tiny[0:1, 1:2], tiny[0:1, 1:2], -MAGIC, 65535.0, ALU.add, ALU.min)
        e.drain()
        e.tensor_tensor(tiny[0:1, 2:3], tiny[0:1, 0:1], tiny[0:1, 1:2], ALU.mult)
        e.drain()
        e.tensor_tensor(tiny[0:1, 2:3], tiny[0:1, 2:3], tiny[0:1, 12:13], ALU.mult)
        e.drain()
        e.tensor_single_scalar(tiny[0:1, 2:3], tiny[0:1, 2:3], 1.0 / 65536.0, ALU.mult)
        e.drain()
        e.tensor_scalar(tiny[0:1, 2:3], tiny[0:1, 2:3], 1.0 / QMAX, 1e-12,
                        ALU.mult, ALU.max)
        recip1(e, tiny[0:1, 13:14], tiny[0:1, 2:3])
        e.drain()
        e.tensor_tensor(tiny[0:1, 13:14], tiny[0:1, 13:14], tiny[0:1, 12:13], ALU.mult)
        e.drain()
        e.tensor_single_scalar(tiny[0:1, 13:14], tiny[0:1, 13:14], 1.0 / 65536.0,
                               ALU.mult).then_inc(VC.sem, 1)
    VC.n += 1
    m3c = VC.n
    bc2 = bcast_scalars(11, 4, 2, m3c)

    p4.close()
    s34.close()
    if _PHASES < 5:
        return finish_dummy((s46,))
    barrier()

    p5 = ExitStack()
    wdnT = p5.enter_context(nc.sbuf_tensor("wdnT", [128, NCC * 8 * 128], BF16))
    upin = p5.enter_context(nc.sbuf_tensor("upin", [128, 2 * OWN], BF16))
    scr1 = p5.enter_context(nc.sbuf_tensor("scr1", [128, OWN], F32))
    scr2 = p5.enter_context(nc.sbuf_tensor("scr2", [128, OWN], F32))
    scr3 = p5.enter_context(nc.sbuf_tensor("scr3", [128, OWN], F32))
    sbI = p5.enter_context(nc.sbuf_tensor("sbI", [128, OWN], BF16))
    sbS = p5.enter_context(nc.sbuf_tensor("sbS", [128, OWN], BF16))
    sbU = p5.enter_context(nc.sbuf_tensor("sbU", [128, OWN], BF16))

    @block.vector
    def _(e):
        e.memset(macc[:, 0:2], NEG_PAD).then_inc(VC.sem, 1)
    VC.n += 1
    macc5_z = VC.n

    @block.sync
    def _(e, toks=[(s.sem, s.n) for s in (s_upst[0], s_upst[1])]):
        for tk in toks:
            dwait(e, tk)

    upld_l, s5_l, v5_l, g5_l = [], [], [], []
    for cc in range(NCC):
        b = cc % 2

        @block.sync
        def _(e, cc=cc, b=b):
            if cc >= 2:
                e.wait_ge(GC.sem, g5_l[cc - 2])
            dinc(e.dma_start(out=upin[:, b * OWN:(b + 1) * OWN],
                             in_=upf_dram[cc * 128:(cc + 1) * 128, :]), s_upld[b])
        upld_l.append((s_upld[b].sem, s_upld[b].n))

        @block.scalar
        def _(e, cc=cc, bcw=bc2):
            if cc == 0:
                dwait(e, bcw)
            for d in range(8):
                dinc(e.dma_start(
                    out=wdnT[:, (cc * 8 + d) * 128:(cc * 8 + d + 1) * 128],
                    in_=wdnq_dram[d * 128:(d + 1) * 128, cc * 128:(cc + 1) * 128],
                    transpose=True), c_wd)

        @block.scalar
        def _(e, cc=cc):
            if cc >= 1:
                e.wait_ge(VC.sem, v5_l[cc - 1])
            gg = gate_sb[:, cc * OWN:(cc + 1) * OWN]
            e.activation(scr1[:, :], gg, AF.Identity, scale=scv_sb[:, 2:3],
                         bias=mgc_sb[:, 0:1])
            e.add(sbI[:, :], scr1[:, :], nmg_sb[:, 0:1])
            e.activation(sbS[:, :], sbI[:, :], AF.Sigmoid,
                         scale=scv_sb[:, 3:4]).then_inc(SA.sem, 1)
        SA.n += 1
        s5_l.append(SA.n)

        @block.gpsimd
        def _(e, cc=cc, b=b, w=upld_l[cc]):
            dwait(e, w)
            if cc >= 1:
                e.wait_ge(VC.sem, v5_l[cc - 1])
            e.tensor_scalar(scr3[:, :], upin[:, b * OWN:(b + 1) * OWN],
                            scv_sb[:, 5:6], MAGIC, ALU.mult, ALU.add)
            e.tensor_single_scalar(sbU[:, :], scr3[:, :], -MAGIC,
                                   ALU.add).then_inc(GC.sem, 1)
        GC.n += 1
        g5_l.append(GC.n)

        @block.vector
        def _(e, cc=cc, ws=s5_l[cc], wg=g5_l[cc], mz=macc5_z):
            e.wait_ge(SA.sem, ws)
            if cc == 0:
                e.wait_ge(VC.sem, mz)
            e.tensor_scalar(scr2[:, :], sbS[:, :], 65536.0, MAGIC, ALU.mult, ALU.add)
            e.tensor_scalar(sbS[:, :], scr2[:, :], -MAGIC, 65535.0, ALU.add, ALU.min)
            e.tensor_tensor(sbI[:, :], sbI[:, :], sbS[:, :], ALU.mult)
            e.tensor_scalar(scr2[:, :], sbI[:, :], scv_sb[:, 4:5], MAGIC,
                            ALU.mult, ALU.add)
            e.wait_ge(GC.sem, wg)
            gg = gate_sb[:, cc * OWN:(cc + 1) * OWN]
            e.scalar_tensor_tensor(gg, scr2[:, :], -MAGIC, sbU[:, :],
                                   ALU.add, ALU.mult)
            e.reduce_max(macc[:, 1:2], gg, axis=AXX, apply_absolute_value=True)
            e.tensor_tensor(macc[:, 0:1], macc[:, 0:1], macc[:, 1:2],
                            ALU.max).then_inc(VC.sem, 1)
        VC.n += 1
        v5_l.append(VC.n)

    pm4 = part_max(0, 0, v5_l[-1])
    ar3 = allreduce_max(2, 0, 1, pm4)

    @block.vector
    def _(e, w=ar3):
        dwait(e, w)
        e.tensor_tensor(tiny[0:1, 1:2], tiny[0:1, 2:3], tiny[0:1, 3:4], ALU.mult)
        e.drain()
        e.tensor_tensor(tiny[0:1, 5:6], tiny[0:1, 0:1], tiny[0:1, 1:2], ALU.mult)
        e.drain()
        e.tensor_scalar(tiny[0:1, 7:8], tiny[0:1, 5:6], 1.0 / QMAX, 1e-12,
                        ALU.mult, ALU.max)
        recip1(e, tiny[0:1, 6:7], tiny[0:1, 7:8])
        e.drain()
        e.tensor_tensor(tiny[0:1, 6:7], tiny[0:1, 6:7], tiny[0:1, 1:2],
                        ALU.mult).then_inc(VC.sem, 1)
    VC.n += 1
    s4_v = VC.n
    bc3 = bcast_scalars(6, 2, 6, s4_v)

    @block.vector
    def _(e, w=bc3):
        dwait(e, w)
        e.tensor_tensor(gbs4_sb[:, :], gb_sb[:, :],
                        scv_sb[:, 7:8].broadcast_to([128, 8]), ALU.mult).then_inc(VC.sem, 1)
    VC.n += 1
    gbs4_v = VC.n

    oi_l = []
    for cc in range(NCC):
        @block.scalar
        def _(e, cc=cc, gw=gbs4_v):
            if cc == 0:
                e.wait_ge(VC.sem, gw)
            if cc >= 1:
                e.wait_ge(VC.sem, oi_l[cc - 1])
            gg = gate_sb[:, cc * OWN:(cc + 1) * OWN]
            e.activation(scr1[:, :], gg, AF.Identity, scale=scv_sb[:, 6:7],
                         bias=mgc_sb[:, 0:1]).then_inc(SA.sem, 1)
        SA.n += 1
        so = SA.n

        @block.vector
        def _(e, cc=cc, so=so):
            e.wait_ge(SA.sem, so)
            gg = gate_sb[:, cc * OWN:(cc + 1) * OWN]
            e.tensor_single_scalar(gg, scr1[:, :], -MAGIC, ALU.add).then_inc(VC.sem, 1)
        VC.n += 1
        oi_l.append(VC.n)

    wdnT_all = (c_wd.sem, c_wd.n)
    if _PHASES < 6:
        return finish_dummy((p5, s46))
    p5_close_after = oi_l[-1]
    barrier()

    p6 = ExitStack()
    x2d = p6.enter_context(nc.sbuf_tensor("x2d", [128, 2 * 512], BF16))
    outt = p6.enter_context(nc.sbuf_tensor("outt", [128, 2 * 512], F32))

    dn_l, ep_l, st_l, x2l_l = [], [], [], []
    for Dc in range(8):
        psD = (pA[:, :], pB[:, :], pC[:, :], pD[:, :]) if Dc % 2 == 0 else \
            (ring0[:, 0:512], ring0[:, 512:1024], ring1[:, 0:512], ring1[:, 512:1024])

        @block.tensor
        def _(e, Dc=Dc, psD=psD, ww=wdnT_all, ow=oi_l[-1]):
            if Dc == 0:
                dwait(e, ww)
                e.wait_ge(VC.sem, ow)
            if Dc >= 2:
                e.wait_ge(VC.sem, ep_l[4 * (Dc - 2) + 3])
            for cc in range(NCC):
                wt = wdnT[:, (cc * 8 + Dc) * 128:(cc * 8 + Dc + 1) * 128]
                for tt in range(4):
                    ins = e.matmul(
                        psD[tt], wt,
                        gate_sb[:, cc * OWN + tt * 512:cc * OWN + (tt + 1) * 512],
                        start=(cc == 0), stop=(cc == NCC - 1))
            ins.then_inc(TM.sem, 1)
        TM.n += 1
        dn_l.append(TM.n)

        for tt in range(4):
            ib = (Dc * 4 + tt) % 2

            @block.sync
            def _(e, Dc=Dc, tt=tt, ib=ib):
                if Dc * 4 + tt >= 2:
                    e.wait_ge(VC.sem, ep_l[Dc * 4 + tt - 2])
                dinc(e.dma_start(
                    out=x2d[:, ib * 512:(ib + 1) * 512],
                    in_=x2_dram[:, Dc * OWN + tt * 512:Dc * OWN + (tt + 1) * 512]),
                    s_x2d[ib])
            x2l_l.append((s_x2d[ib].sem, s_x2d[ib].n))

            @block.vector
            def _(e, Dc=Dc, tt=tt, ib=ib, psD=psD, xl=x2l_l[-1], dw=dn_l[Dc]):
                e.wait_ge(TM.sem, dw)
                dwait(e, xl)
                if Dc * 4 + tt >= 2:
                    dwait(e, st_l[Dc * 4 + tt - 2])
                e.scalar_tensor_tensor(outt[:, ib * 512:(ib + 1) * 512], psD[tt],
                                       gbs4_sb[:, Dc:Dc + 1],
                                       x2d[:, ib * 512:(ib + 1) * 512],
                                       ALU.mult, ALU.add).then_inc(VC.sem, 1)
            VC.n += 1
            ep_l.append(VC.n)

            @block.sync
            def _(e, Dc=Dc, tt=tt, ib=ib, w=ep_l[-1]):
                e.wait_ge(VC.sem, w)
                dinc(e.dma_start(
                    out=out[Dc * 128:(Dc + 1) * 128, tt * 512:(tt + 1) * 512],
                    in_=outt[:, ib * 512:(ib + 1) * 512]), s_out[ib])
            st_l.append((s_out[ib].sem, s_out[ib].n))

    if os.environ.get("K_DUMP", "") == "scv":
        @block.vector
        def _(e):
            e.tensor_copy(outt[:, 0:16], scv_sb[:, :])
            e.tensor_copy(outt[0:1, 16:32], tiny[0:1, :])
            e.tensor_copy(outt[0:1, 32:96], dbg_sb[0:1, :]).then_inc(VC.sem, 1)
        VC.n += 1

        @block.sync
        def _(e, wv=VC.n, tok0=(s_out[0].sem, s_out[0].n), tok1=(s_out[1].sem, s_out[1].n)):
            dwait(e, tok0)
            dwait(e, tok1)
            e.wait_ge(VC.sem, wv)
            dinc(e.dma_start(out=out[0:128, 0:96], in_=outt[:, 0:96]), s_out[0])

    @block.sync
    def _(e, toks=[(s.sem, s.n) for s in (s_out[0], s_out[1])]):
        for tk in toks:
            dwait(e, tk)

    p6.close()
    p5.close()
    s46.close()
    es.close()
    return nc


_NC_CACHE = {}


def _get_nc():
    if "nc" not in _NC_CACHE:
        _NC_CACHE["nc"] = build_nc()
    return _NC_CACHE["nc"]


def kernel(L0_attn_x, L0_attn_qkv_weights, L0_ffn_w_gate, L0_ffn_w_up,
           L0_ffn_w_down, L0_c, L0_wmod):
    x = np.ascontiguousarray(np.asarray(L0_attn_x, dtype=np.float32))
    nc = _get_nc()

    eshift_np = np.zeros((64, 128), dtype=np.float32)
    eshift_np[np.arange(64), 64 + np.arange(64)] = 1.0


    wq = np.ascontiguousarray(np.asarray(L0_attn_qkv_weights, dtype=np.float32))
    wm = np.ascontiguousarray(np.asarray(L0_wmod, dtype=np.float32))
    wu = np.ascontiguousarray(np.asarray(L0_ffn_w_up, dtype=np.float32))
    wg = np.ascontiguousarray(np.asarray(L0_ffn_w_gate, dtype=np.float32))
    wd = np.ascontiguousarray(np.asarray(L0_ffn_w_down, dtype=np.float32))
    cc = np.asarray(L0_c, dtype=np.float32)

    in_maps = []
    ident_np = np.eye(128, dtype=np.float32)
    for c in range(8):
        b, par = c // 2, c % 2
        xT_ = np.ascontiguousarray(x[b].T)
        xoT_ = np.ascontiguousarray(x[b, par::2].T)
        cvec_ = np.ascontiguousarray(cc[b].reshape(8, 128).T)
        kl = np.arange(128)
        qg = 2 * np.arange(64) + par
        dm = np.where(kl[:, None] <= qg[None, :], 0.0, NEG).astype(np.float32)
        in_maps.append({
            "xT": xT_, "xoT": xoT_, "wqkv": wq, "wmod": wm, "cvec": cvec_,
            "wup_n": wu, "wgate_n": wg, "wdn_n": wd,
            "dmask": dm, "eshift": eshift_np, "ident": ident_np,
        })

    res = run_bass_kernel_spmd(nc, in_maps, core_ids=list(range(8)))
    outp = np.empty_like(x)
    for c in range(8):
        b, par = c // 2, c % 2
        outp[b, par::2, :] = res.results[c]["out"].T
    return outp


# version 08 (final)
# speedup vs baseline: 1.0852x; 1.0852x over previous
import os
import numpy as np
from contextlib import ExitStack

import concourse.bass as bass
import concourse.mybir as mybir
from concourse.bass_utils import run_bass_kernel_spmd

F32 = mybir.dt.float32
BF16 = mybir.dt.bfloat16
AF = mybir.ActivationFunctionType
ALU = mybir.AluOpType
AXX = mybir.AxisListType.X

MAGIC = 12582912.0
NEG = -1.0e9
NEG_PAD = -1.0e30
D = 1024
S = 4096
OWN = 2048
NPAIR = 8
I3 = 3072
NCC = 24
QMAX = 32767.0
EPS = 1e-5


class Ctr:
    __slots__ = ("sem", "n")

    def __init__(self, sem):
        self.sem = sem
        self.n = 0


_PHASES = int(os.environ.get("K_PHASES", "9"))
_SUB = int(os.environ.get("K_SUB", "9"))


def build_nc():
    nc = bass.Bass("TRN2", target_bir_lowering=False,
                   detect_race_conditions=False)

    xT = nc.dram_tensor("xT", [D, S], F32, kind="ExternalInput")
    xoT = nc.dram_tensor("xoT", [D, OWN], F32, kind="ExternalInput")
    wqkv = nc.dram_tensor("wqkv", [D, 3 * D], F32, kind="ExternalInput")
    wmod = nc.dram_tensor("wmod", [D, 6 * D], F32, kind="ExternalInput")
    cvec = nc.dram_tensor("cvec", [128, 8], F32, kind="ExternalInput")
    wup_n = nc.dram_tensor("wup_n", [I3, D], F32, kind="ExternalInput")
    wgate_n = nc.dram_tensor("wgate_n", [I3, D], F32, kind="ExternalInput")
    wdn_n = nc.dram_tensor("wdn_n", [D, I3], F32, kind="ExternalInput")
    dmask = nc.dram_tensor("dmask", [128, 64], F32, kind="ExternalInput")
    eshift = nc.dram_tensor("eshift", [64, 128], F32, kind="ExternalInput")
    ident = nc.dram_tensor("ident", [128, 128], F32, kind="ExternalInput")
    out = nc.dram_tensor("out", [D, OWN], F32, kind="ExternalOutput")

    mod_dram = nc.dram_tensor("mod_dram", [1, 6 * D], F32, kind="Internal")
    h1f_dram = nc.dram_tensor("h1f_dram", [128, 8 * S], BF16, kind="Internal")
    x2_dram = nc.dram_tensor("x2_dram", [128, 8 * OWN], BF16, kind="Internal")
    q_dram = nc.dram_tensor("q_dram", [128, NPAIR * OWN], BF16, kind="Internal")
    upf_dram = nc.dram_tensor("upf_dram", [I3, OWN], BF16, kind="Internal")
    wupq_dram = nc.dram_tensor("wupq_dram", [I3, D], BF16, kind="Internal")
    wgq_dram = nc.dram_tensor("wgq_dram", [I3, D], BF16, kind="Internal")
    wdnq_dram = nc.dram_tensor("wdnq_dram", [D, I3], BF16, kind="Internal")
    scal_dram = nc.dram_tensor("scal_dram", [1, 16], F32, kind="Internal")
    bnc_dram = nc.dram_tensor("bnc_dram", [1, 128], F32, kind="Internal")
    cc_in = [nc.dram_tensor(f"cc_in{i}", [1, 8], F32, kind="Internal") for i in range(3)]
    cc_out = [nc.dram_tensor(f"cc_out{i}", [1, 8], F32, kind="Internal") for i in range(3)]

    es = ExitStack()
    E = es.enter_context

    TM = Ctr(E(nc.semaphore("TM")))
    VC = Ctr(E(nc.semaphore("VC")))
    SA = Ctr(E(nc.semaphore("SA")))
    GC = Ctr(E(nc.semaphore("GC")))
    CCs = Ctr(E(nc.semaphore("CCs")))
    PB = Ctr(E(nc.semaphore("PB")))

    def mkds(name, k=2):
        return [Ctr(E(nc.semaphore(f"{name}{i}"))) for i in range(k)]

    s_cons = mkds("s_cons", 1)[0]
    s_wmod = mkds("s_wmod", 1)[0]
    s_util = mkds("s_util", 1)[0]
    g_es = mkds("g_es", 1)[0]
    g_ar = mkds("g_ar", 1)[0]
    s_h1sp = mkds("s_h1sp")
    s_qsp = mkds("s_qsp")
    g_x = mkds("g_x")
    g_w = mkds("g_w")
    g_wk = mkds("g_wk")
    s_h1ld = mkds("s_h1ld")
    s_qpld = mkds("s_qpld")
    s_xot = mkds("s_xot")
    s_x2st = mkds("s_x2st")
    s_dqld = mkds("s_dqld")
    s_dqst = mkds("s_dqst")
    s_bias = mkds("s_bias")
    s_x2ld = mkds("s_x2ld")
    s_upst = mkds("s_upst")
    s_upld = mkds("s_upld")
    s_x2d = mkds("s_x2d")
    s_out = mkds("s_out")
    c_w = mkds("c_w")
    c_wd = mkds("c_wd", 1)[0]

    def dinc(e_ins, ds):
        e_ins.then_inc(ds.sem, 16)
        ds.n += 16
        return (ds.sem, ds.n)

    def dwait(e, tok):
        e.wait_ge(tok[0], tok[1])

    def recip1(e, dst, srcp):
        e.drain()
        e.tensor_copy(tinyB[0:1, 120:121], srcp)
        e.tensor_copy(tinyB[0:1, 121:122], srcp)
        e.drain()
        e.reciprocal(tinyB[0:1, 122:124], tinyB[0:1, 120:122])
        e.drain()
        return e.tensor_copy(dst, tinyB[0:1, 122:123])

    pA = E(nc.psum_tensor("pA", [128, 512], F32))
    pB = E(nc.psum_tensor("pB", [128, 512], F32))
    pC = E(nc.psum_tensor("pC", [128, 512], F32))
    pD = E(nc.psum_tensor("pD", [128, 512], F32))
    ring0 = E(nc.psum_tensor("ring0", [128, 1024], F32))
    ring1 = E(nc.psum_tensor("ring1", [128, 1024], F32))

    ones_bf = E(nc.sbuf_tensor("ones_bf", [128, 128], BF16))
    ones_f = E(nc.sbuf_tensor("ones_f", [1, 64], F32))
    dmask_sb = E(nc.sbuf_tensor("dmask_sb", [128, 64], F32))
    eshift_sb = E(nc.sbuf_tensor("eshift_sb", [64, 128], BF16))
    ident_sb = E(nc.sbuf_tensor("ident_sb", [128, 128], BF16))
    ca1_sb = E(nc.sbuf_tensor("ca1_sb", [128, 8], F32))
    sa_sb = E(nc.sbuf_tensor("sa_sb", [128, 8], F32))
    ga_sb = E(nc.sbuf_tensor("ga_sb", [128, 8], F32))
    gaup_sb = E(nc.sbuf_tensor("gaup_sb", [64, 8], F32))
    cb1_sb = E(nc.sbuf_tensor("cb1_sb", [128, 8], F32))
    sbv_sb = E(nc.sbuf_tensor("sbv_sb", [128, 8], F32))
    gb_sb = E(nc.sbuf_tensor("gb_sb", [128, 8], F32))
    gbs4_sb = E(nc.sbuf_tensor("gbs4_sb", [128, 8], F32))
    silu_sb = E(nc.sbuf_tensor("silu_sb", [128, 8], F32))
    bias_sb = E(nc.sbuf_tensor("bias_sb", [128, 2], F32))
    scv_sb = E(nc.sbuf_tensor("scv_sb", [128, 16], F32))
    tiny = E(nc.sbuf_tensor("tiny", [1, 16], F32))
    tinyB = E(nc.sbuf_tensor("tinyB", [1, 128], F32))
    dbg_sb = E(nc.sbuf_tensor("dbg_sb", [1, 64], F32))
    macc = E(nc.sbuf_tensor("macc", [128, 8], F32))
    eps_sb = E(nc.sbuf_tensor("eps_sb", [128, 1], F32))
    mgc_sb = E(nc.sbuf_tensor("mgc_sb", [128, 1], F32))
    nmg_sb = E(nc.sbuf_tensor("nmg_sb", [128, 1], F32))

    block = E(nc.Block())

    def barrier():
        tgt = PB.n + 5
        for dec in (block.sync, block.gpsimd, block.tensor, block.vector, block.scalar):
            @dec
            def _(e, tgt=tgt):
                e.drain().then_inc(PB.sem, 1)
                e.wait_ge(PB.sem, tgt)
        PB.n = tgt

    def part_max(col, dst, wait_vc):
        @block.sync
        def _(e, col=col, wv=wait_vc):
            e.wait_ge(VC.sem, wv)
            dinc(e.dma_start(out=bnc_dram[0:1, :], in_=macc[:, col:col + 1]), s_util)
        w1 = (s_util.sem, s_util.n)

        @block.sync
        def _(e, w1=w1):
            dwait(e, w1)
            dinc(e.dma_start(out=tinyB[0:1, :], in_=bnc_dram[0:1, :]), s_util)
        w2 = (s_util.sem, s_util.n)

        @block.vector
        def _(e, dst=dst, w2=w2):
            dwait(e, w2)
            e.reduce_max(tiny[0:1, dst:dst + 1], tinyB[0:1, :], axis=AXX).then_inc(VC.sem, 1)
        VC.n += 1
        return VC.n

    def allreduce_max(idx, c0, n, wait_vc):
        @block.vector
        def _(e, c0=c0, n=n, wv=wait_vc):
            e.wait_ge(VC.sem, wv)
            e.memset(tinyB[0:1, 0:8], NEG_PAD)
            e.drain()
            e.tensor_copy(tinyB[0:1, 0:n], tiny[0:1, c0:c0 + n])
            e.drain().then_inc(VC.sem, 1)
        VC.n += 1
        wv2 = VC.n

        @block.gpsimd
        def _(e, idx=idx, wv2=wv2):
            e.wait_ge(VC.sem, wv2)
            dinc(e.dma_start(out=cc_in[idx][0:1, :], in_=tinyB[0:1, 0:8]), g_ar)
        w1 = (g_ar.sem, g_ar.n)

        @block.gpsimd
        def _(e, idx=idx, w1=w1):
            dwait(e, w1)
            e.collective_compute(
                "AllReduce", ALU.max, replica_groups=[list(range(8))],
                ins=[cc_in[idx].ap()], outs=[cc_out[idx].ap()],
            ).then_inc(CCs.sem, 1)
        CCs.n += 1
        wc = CCs.n

        @block.gpsimd
        def _(e, idx=idx, c0=c0, n=n, wc=wc):
            e.wait_ge(CCs.sem, wc)
            dinc(e.dma_start(out=tiny[0:1, c0:c0 + n], in_=cc_out[idx][0:1, 0:n]), g_ar)
        return (g_ar.sem, g_ar.n)

    def bcast_scalars(c0, n, d0, wait_vc):
        @block.sync
        def _(e, c0=c0, n=n, wv=wait_vc):
            e.wait_ge(VC.sem, wv)
            dinc(e.dma_start(out=scal_dram[0:1, 0:n], in_=tiny[0:1, c0:c0 + n]), s_util)
        w1 = (s_util.sem, s_util.n)

        @block.sync
        def _(e, n=n, d0=d0, w1=w1):
            dwait(e, w1)
            for j in range(n):
                dinc(e.dma_start(out=scv_sb[:, d0 + j:d0 + j + 1],
                                 in_=scal_dram[0:1, j:j + 1].partition_broadcast(128)),
                     s_util)
        return (s_util.sem, s_util.n)

    s12 = ExitStack()
    V_all = s12.enter_context(nc.sbuf_tensor("V_all", [128, 32 * 8 * 130], BF16))

    s01 = ExitStack()
    wmod_sb = s01.enter_context(nc.sbuf_tensor("wmod_sb", [128, 8 * 512], F32))
    modrow = s01.enter_context(nc.sbuf_tensor("modrow", [1, 6 * D], F32))
    modtmp = s01.enter_context(nc.sbuf_tensor("modtmp", [1, 2 * D], F32))

    @block.sync
    def _(e):
        dinc(e.dma_start(out=dmask_sb[:, :], in_=dmask[:, :]), s_cons)
        dinc(e.dma_start(out=silu_sb[:, :], in_=cvec[:, :]), s_cons)
    sd_consts = (s_cons.sem, s_cons.n)

    @block.gpsimd
    def _(e):
        dinc(e.dma_start(out=eshift_sb[:, :], in_=eshift[:, :]), g_es)
        dinc(e.dma_start(out=ident_sb[:, :], in_=ident[:, :]), g_es)
    gd_eshift = (g_es.sem, g_es.n)

    @block.vector
    def _(e):
        e.memset(ones_bf[:, :], 1.0)
        e.memset(eps_sb[:, 0:1], EPS)
        e.memset(mgc_sb[:, 0:1], MAGIC)
        e.memset(nmg_sb[:, 0:1], -MAGIC)
        e.memset(ones_f[0:1, :], 1.0).then_inc(VC.sem, 1)
    VC.n += 1
    vc_ones = VC.n

    @block.scalar
    def _(e, w=sd_consts):
        dwait(e, w)
        e.activation(silu_sb[:, :], silu_sb[:, :], AF.Silu).then_inc(SA.sem, 1)
    SA.n += 1
    sa_silu = SA.n

    def finish_dummy(close_stacks=()):
        dump = os.environ.get("K_DUMP", "")
        with nc.sbuf_tensor("dummy_out", [128, 2048], F32) as dout, \
             nc.sbuf_tensor("dummy_bf", [128, 2048], BF16) as dbf:
            if dump:
                dsrc = {"x2": x2_dram, "h1": h1f_dram, "upf": upf_dram}[dump]
                for c2 in range(8):
                    @block.sync
                    def _(e, c2=c2, dump=dump, tok=(s_out[1].sem, s_out[1].n)):
                        dwait(e, tok)
                        srcap = (dsrc[c2 * 128:(c2 + 1) * 128, :] if dump == "upf"
                                 else dsrc[:, c2 * 2048:(c2 + 1) * 2048])
                        dinc(e.dma_start(out=dbf[:, :], in_=srcap), s_out[1])

                    @block.vector
                    def _(e, tok=(s_out[1].sem, s_out[1].n)):
                        dwait(e, tok)
                        e.tensor_copy(dout[:, :], dbf[:, :]).then_inc(VC.sem, 1)
                    VC.n += 1

                    @block.sync
                    def _(e, c2=c2, wv=VC.n):
                        e.wait_ge(VC.sem, wv)
                        dinc(e.dma_start(out=out[c2 * 128:(c2 + 1) * 128, :],
                                         in_=dout[:, :]), s_out[0])
                @block.sync
                def _(e, tok=(s_out[0].sem, s_out[0].n)):
                    dwait(e, tok)
            else:
                @block.vector
                def _(e):
                    e.memset(dout[:, :], 0.0).then_inc(VC.sem, 1)
                VC.n += 1
                wv = VC.n

                @block.sync
                def _(e, wv=wv):
                    e.wait_ge(VC.sem, wv)
                    for Dc in range(8):
                        dinc(e.dma_start(out=out[Dc * 128:(Dc + 1) * 128, :],
                                         in_=dout[:, :]), s_out[0])

                @block.sync
                def _(e, tok=(s_out[0].sem, s_out[0].n)):
                    dwait(e, tok)
        for stk in close_stacks:
            stk.close()
        es.close()
        return nc

    if _SUB < 1:
        return finish_dummy((s01, s12))
    mv_mm, mv_ev, mv_ld = [], [], []
    for n in range(12):
        @block.sync
        def _(e, n=n):
            if n >= 1:
                e.wait_ge(TM.sem, mv_mm[n - 1])
            for j in range(8):
                dinc(e.dma_start(out=wmod_sb[:, j * 512:(j + 1) * 512],
                                 in_=wmod[j * 128:(j + 1) * 128, n * 512:(n + 1) * 512]),
                     s_wmod)
        mv_ld.append((s_wmod.sem, s_wmod.n))

        @block.tensor
        def _(e, n=n, ld=mv_ld[n], silu_w=sa_silu):
            dwait(e, ld)
            e.wait_ge(SA.sem, silu_w)
            if n >= 1:
                e.wait_ge(VC.sem, mv_ev[n - 1])
            for j in range(8):
                ins = e.matmul(ring1[0:1, 0:512], silu_sb[:, j:j + 1],
                               wmod_sb[:, j * 512:(j + 1) * 512],
                               start=(j == 0), stop=(j == 7))
            ins.then_inc(TM.sem, 1)
        TM.n += 1
        mv_mm.append(TM.n)

        @block.vector
        def _(e, n=n, mm=mv_mm[n]):
            e.wait_ge(TM.sem, mm)
            e.tensor_copy(modrow[0:1, n * 512:(n + 1) * 512],
                          ring1[0:1, 0:512]).then_inc(VC.sem, 1)
        VC.n += 1
        mv_ev.append(VC.n)

    if _SUB < 2:
        return finish_dummy((s01, s12))
    @block.vector
    def _(e, w=mv_ev[-1]):
        e.wait_ge(VC.sem, w)
        e.tensor_single_scalar(modrow[0:1, 1024:2048], modrow[0:1, 1024:2048], 1.0, ALU.add)
        e.tensor_single_scalar(modrow[0:1, 4096:5120], modrow[0:1, 4096:5120], 1.0, ALU.add)
        for base in (2048, 5120):
            ga = modrow[0:1, base:base + 1024]
            sgn = modtmp[0:1, 0:1024]
            nga = modtmp[0:1, 1024:2048]
            e.tensor_scalar(sgn, ga, 0.0, None, ALU.is_ge)
            e.tensor_scalar(sgn, sgn, 2.0, -1.0, ALU.mult, ALU.add)
            e.tensor_single_scalar(nga, ga, -1.0, ALU.mult)
            e.tensor_tensor(ga, ga, nga, ALU.max)
            e.tensor_single_scalar(ga, ga, 0.25, ALU.max)
            ins = e.tensor_tensor(ga, ga, sgn, ALU.mult)
        ins.then_inc(VC.sem, 1)
    VC.n += 1
    vc_modpost = VC.n

    if _SUB < 3:
        return finish_dummy((s01, s12))
    @block.sync
    def _(e, w=vc_modpost):
        e.wait_ge(VC.sem, w)
        dinc(e.dma_start(out=mod_dram[0:1, :], in_=modrow[0:1, :]), s_util)
    sd_modout = (s_util.sem, s_util.n)

    mod4 = mod_dram.ap().rearrange("o (m c p) -> o m p c", m=6, c=8)

    @block.sync
    def _(e, w=sd_modout):
        dwait(e, w)
        for dst, src_ap in ((sa_sb[:, :], mod4[0:1, 0, :, :]),
                            (ca1_sb[:, :], mod4[0:1, 1, :, :]),
                            (ga_sb[:, :], mod4[0:1, 2, :, :]),
                            (gaup_sb[:, :], mod4[0:1, 2, 64:128, :]),
                            (sbv_sb[:, :], mod4[0:1, 3, :, :]),
                            (cb1_sb[:, :], mod4[0:1, 4, :, :]),
                            (gb_sb[:, :], mod4[0:1, 5, :, :])):
            dinc(e.dma_start(out=dst, in_=src_ap,
                             allow_slow_non_contiguous=True), s_util)
    sd_modvec = (s_util.sem, s_util.n)

    if _PHASES < 1:
        return finish_dummy((s01, s12))

    p1 = ExitStack()
    xbf = p1.enter_context(nc.sbuf_tensor("xbf", [128, 2 * 8 * 512], BF16))
    xsq = p1.enter_context(nc.sbuf_tensor("xsq", [128, 8 * 512], BF16))
    h1t = p1.enter_context(nc.sbuf_tensor("h1t", [128, 2 * 8 * 512], BF16))
    mrep = p1.enter_context(nc.sbuf_tensor("mrep", [128, 512], F32))
    rrep = p1.enter_context(nc.sbuf_tensor("rrep", [128, 512], F32))
    wslab = p1.enter_context(nc.sbuf_tensor("wslab", [128, 2 * 8 * 1024], BF16))
    qt_sb = p1.enter_context(nc.sbuf_tensor("qt_sb", [128, 2 * 512], BF16))

    @block.vector
    def _(e):
        v4 = V_all[:, :].rearrange("p (kc sl c) -> p kc sl c", kc=32, sl=16)
        e.memset(v4[:, :, :, 64:65], 1.0).then_inc(VC.sem, 1)
    VC.n += 1
    vc_vones = VC.n

    xcast_l, wsl_l, vev_all = [], [], []
    h1_l, vproj_l, h1sp_l, stats_l, rstd_l, xsq_l = [], [], [], [], [], []
    slab_done = []
    h1t_done = []
    qev_last = []

    for t in range(12):
        tb = t % 2
        full = t < 8

        @block.gpsimd
        def _(e, t=t, tb=tb, full=full):
            if t >= 2:
                e.wait_ge(VC.sem, h1_l[t - 2])
            for d in range(8):
                src = (xT[d * 128:(d + 1) * 128, t * 512:(t + 1) * 512] if full else
                       xoT[d * 128:(d + 1) * 128, (t - 8) * 512:(t - 7) * 512])
                dinc(e.dma_start(out=xbf[:, (tb * 8 + d) * 512:(tb * 8 + d + 1) * 512],
                                 in_=src), g_x[tb])
        xcast_l.append((g_x[tb].sem, g_x[tb].n))

        @block.gpsimd
        def _(e, t=t, tb=tb, full=full):
            if t >= 2:
                e.wait_ge(TM.sem, slab_done[t - 2])
            cols = (2048, 3072) if full else (0, 1024)
            for d in range(8):
                dinc(e.dma_start(out=wslab[:, (tb * 8 + d) * 1024:(tb * 8 + d + 1) * 1024],
                                 in_=wqkv[d * 128:(d + 1) * 128, cols[0]:cols[1]]),
                     g_w[tb])
        wsl_l.append((g_w[tb].sem, g_w[tb].n))

        @block.vector
        def _(e, t=t, tb=tb):
            dwait(e, xcast_l[t])
            if t >= 1:
                e.wait_ge(TM.sem, stats_l[t - 1])
            for d in range(8):
                xb = xbf[:, (tb * 8 + d) * 512:(tb * 8 + d + 1) * 512]
                ins = e.tensor_tensor(xsq[:, d * 512:(d + 1) * 512], xb, xb, ALU.mult)
            ins.then_inc(VC.sem, 1)
        VC.n += 1
        xsq_l.append(VC.n)

        @block.tensor
        def _(e, t=t, tb=tb, ws=xsq_l[t], oz=vc_ones):
            e.wait_ge(VC.sem, ws)
            e.wait_ge(VC.sem, oz)
            if t >= 1:
                e.wait_ge(VC.sem, rstd_l[t - 1])
            for d in range(8):
                xb = xbf[:, (tb * 8 + d) * 512:(tb * 8 + d + 1) * 512]
                e.matmul(pA[:, :], ones_bf[:, :], xb, start=(d == 0), stop=(d == 7))
            for d in range(8):
                ins = e.matmul(pB[:, :], ones_bf[:, :], xsq[:, d * 512:(d + 1) * 512],
                               start=(d == 0), stop=(d == 7))
            ins.then_inc(TM.sem, 1)
        TM.n += 1
        stats_l.append(TM.n)

        @block.vector
        def _(e, t=t, w=stats_l[t]):
            e.wait_ge(TM.sem, w)
            e.tensor_single_scalar(mrep[:, :], pA[:, :], 1.0 / 1024.0, ALU.mult)
            e.tensor_tensor(rrep[:, :], mrep[:, :], mrep[:, :], ALU.mult)
            e.scalar_tensor_tensor(rrep[:, :], pB[:, :], 1.0 / 1024.0, rrep[:, :],
                                   ALU.mult, ALU.subtract).then_inc(VC.sem, 1)
        VC.n += 1
        var_v = VC.n

        @block.scalar
        def _(e, w=var_v):
            e.wait_ge(VC.sem, w)
            e.activation(rrep[:, :], rrep[:, :], AF.Sqrt,
                         bias=eps_sb[:, 0:1]).then_inc(SA.sem, 1)
        SA.n += 1
        sq_v = SA.n

        @block.vector
        def _(e, w=sq_v):
            e.wait_ge(SA.sem, w)
            e.reciprocal(rrep[:, :], rrep[:, :]).then_inc(VC.sem, 1)
            VC.n += 1
            e.wait_ge(VC.sem, VC.n)
        rstd_l.append(VC.n)

        @block.vector
        def _(e, t=t, tb=tb, mv=sd_modvec):
            dwait(e, mv)
            if t >= 2:
                for kind, val in h1t_done[t - 2]:
                    if kind == "TM":
                        e.wait_ge(TM.sem, val)
                    elif kind == "SA":
                        e.wait_ge(SA.sem, val)
                    else:
                        dwait(e, val)
            for d in range(8):
                xb = xbf[:, (tb * 8 + d) * 512:(tb * 8 + d + 1) * 512]
                h1 = h1t[:, (tb * 8 + d) * 512:(tb * 8 + d + 1) * 512]
                e.tensor_tensor(h1, xb, mrep[:, :], ALU.subtract)
                e.tensor_tensor(h1, h1, rrep[:, :], ALU.mult)
                ins = e.tensor_scalar(h1, h1, ca1_sb[:, d:d + 1], sa_sb[:, d:d + 1],
                                      ALU.mult, ALU.add)
            ins.then_inc(VC.sem, 1)
        VC.n += 1
        h1_l.append(VC.n)

        if full:
            @block.sync
            def _(e, t=t, tb=tb, w=h1_l[t]):
                e.wait_ge(VC.sem, w)
                for d in range(8):
                    dinc(e.dma_start(
                        out=h1f_dram[:, (d * 8 + t) * 512:(d * 8 + t + 1) * 512],
                        in_=h1t[:, (tb * 8 + d) * 512:(tb * 8 + d + 1) * 512]),
                        s_h1sp[tb])
            h1sp_l.append((s_h1sp[tb].sem, s_h1sp[tb].n))
            spill_tok = h1sp_l[-1]

            for tt in range(4):
                ring = ring0 if tt % 2 == 0 else ring1
                gi = t * 4 + tt

                @block.tensor
                def _(e, t=t, tb=tb, tt=tt, ring=ring, gi=gi, wv=wsl_l[t], hv=h1_l[t]):
                    if tt == 0:
                        dwait(e, wv)
                        e.wait_ge(VC.sem, hv)
                    if gi >= 2:
                        e.wait_ge(VC.sem, vev_all[gi - 2])
                    for d in range(8):
                        h1 = h1t[:, (tb * 8 + d) * 512 + tt * 128:
                                 (tb * 8 + d) * 512 + (tt + 1) * 128]
                        w0 = wslab[:, (tb * 8 + d) * 1024:(tb * 8 + d) * 1024 + 512]
                        w1 = wslab[:, (tb * 8 + d) * 1024 + 512:
                                   (tb * 8 + d) * 1024 + 1024]
                        e.matmul(ring[:, 0:512], h1, w0, start=(d == 0), stop=(d == 7))
                        ins = e.matmul(ring[:, 512:1024], h1, w1, start=(d == 0),
                                       stop=(d == 7))
                    ins.then_inc(TM.sem, 1)
                TM.n += 1
                vmm = TM.n

                @block.vector
                def _(e, t=t, tt=tt, ring=ring, w=vmm):
                    e.wait_ge(TM.sem, w)
                    base = (t * 4 + tt) * 1040
                    dst0 = V_all[:, base:base + 520].rearrange(
                        "p (sl c) -> p sl c", c=65)[:, :, 0:64]
                    dst1 = V_all[:, base + 520:base + 1040].rearrange(
                        "p (sl c) -> p sl c", c=65)[:, :, 0:64]
                    src0 = ring[:, 0:512].rearrange("p (sl c) -> p sl c", c=64)
                    src1 = ring[:, 512:1024].rearrange("p (sl c) -> p sl c", c=64)
                    e.tensor_copy(dst0, src0)
                    e.tensor_copy(dst1, src1).then_inc(VC.sem, 1)
                VC.n += 1
                vev_all.append(VC.n)
            vproj_l.append(TM.n)
            slab_done.append(TM.n)
            h1t_done.append([("TM", TM.n), ("SD", spill_tok)])
        else:
            ot = t - 8
            qmm_sa2, qst_sd2 = [], []
            for p in range(NPAIR):
                ps = pC if p % 2 == 0 else pD

                @block.tensor
                def _(e, t=t, tb=tb, p=p, ps=ps, wq=wsl_l[t], hv=h1_l[t]):
                    if p == 0:
                        dwait(e, wq)
                        e.wait_ge(VC.sem, hv)
                    if p >= 2:
                        e.wait_ge(SA.sem, qmm_sa2[p - 2])
                    elif t >= 9 and p < 2:
                        e.wait_ge(SA.sem, qev_last[-1])
                    for d in range(8):
                        wqt = wslab[:, (tb * 8 + d) * 1024 + p * 128:
                                    (tb * 8 + d) * 1024 + (p + 1) * 128]
                        ins = e.matmul(ps[:, :], wqt,
                                       h1t[:, (tb * 8 + d) * 512:(tb * 8 + d + 1) * 512],
                                       start=(d == 0), stop=(d == 7))
                    ins.then_inc(TM.sem, 1)
                TM.n += 1
                qmm = TM.n

                @block.scalar
                def _(e, p=p, ps=ps, w=qmm):
                    e.wait_ge(TM.sem, w)
                    if p >= 2:
                        dwait(e, qst_sd2[p - 2])
                    e.copy(qt_sb[:, (p % 2) * 512:(p % 2 + 1) * 512],
                           ps[:, :]).then_inc(SA.sem, 1)
                SA.n += 1
                qmm_sa2.append(SA.n)

                @block.sync
                def _(e, ot=ot, p=p, w=qmm_sa2[p]):
                    e.wait_ge(SA.sem, w)
                    dinc(e.dma_start(
                        out=q_dram[:, p * OWN + ot * 512:p * OWN + (ot + 1) * 512],
                        in_=qt_sb[:, (p % 2) * 512:(p % 2 + 1) * 512]),
                        s_qsp[p % 2])
                qst_sd2.append((s_qsp[p % 2].sem, s_qsp[p % 2].n))
            qev_last.append(SA.n)
            slab_done.append(TM.n)
            h1t_done.append([("TM", TM.n)])

    p1.close()
    if _PHASES < 2:
        return finish_dummy((s01, s12))
    barrier()
    s01.close()

    p2 = ExitStack()
    K_p = p2.enter_context(nc.sbuf_tensor("K_p", [128, 2 * S], BF16))
    QA_sb = p2.enter_context(nc.sbuf_tensor("QA_sb", [64, 2 * OWN], BF16))
    QB_sb = p2.enter_context(nc.sbuf_tensor("QB_sb", [64, 2 * OWN], BF16))
    NKT_A = p2.enter_context(nc.sbuf_tensor("NKT_A", [64, S], BF16))
    NKT_B = p2.enter_context(nc.sbuf_tensor("NKT_B", [64, S], BF16))
    P_sb = p2.enter_context(nc.sbuf_tensor("P_sb", [128, 3 * 1024], BF16))
    h1s = p2.enter_context(nc.sbuf_tensor("h1s", [128, 2 * 8 * 512], BF16))
    wks = p2.enter_context(nc.sbuf_tensor("wks", [128, 2 * 8 * 128], BF16))
    ksq_sb = p2.enter_context(nc.sbuf_tensor("ksq_sb", [128, 512], BF16))
    recA = p2.enter_context(nc.sbuf_tensor("recA", [64, 512], F32))
    recB = p2.enter_context(nc.sbuf_tensor("recB", [64, 512], F32))
    u1s = p2.enter_context(nc.sbuf_tensor("u1s", [64, 512], BF16))
    u2s = p2.enter_context(nc.sbuf_tensor("u2s", [64, 512], BF16))
    x2t = p2.enter_context(nc.sbuf_tensor("x2t", [128, 2 * 512], BF16))
    xot = p2.enter_context(nc.sbuf_tensor("xot", [128, 2 * 512], F32))
    dq_in = p2.enter_context(nc.sbuf_tensor("dq_in", [128, 2 * 1024], F32))
    dq_t = p2.enter_context(nc.sbuf_tensor("dq_t", [128, 1024], F32))
    dq_o = p2.enter_context(nc.sbuf_tensor("dq_o", [128, 2 * 1024], BF16))
    dq_s = p2.enter_context(nc.sbuf_tensor("dq_s", [128, 32], F32))
    dq_r = p2.enter_context(nc.sbuf_tensor("dq_r", [128, 32], F32))

    dq_plan = []
    for wsrc, wdst, nrow, ncol in ((wup_n, wupq_dram, I3, D), (wgate_n, wgq_dram, I3, D),
                                   (wdn_n, wdnq_dram, D, I3)):
        for r0 in range(0, nrow, 128):
            for c0 in range(0, ncol, 1024):
                dq_plan.append((wsrc, wdst, r0, c0))
    dq_state = {"i": 0, "g": [], "st": [], "v": []}

    def emit_dequant(k):
        for _ in range(k):
            i = dq_state["i"]
            if i >= len(dq_plan):
                return
            dq_state["i"] += 1
            wsrc, wdst, r0, c0 = dq_plan[i]
            buf = i % 2

            @block.sync
            def _(e, wsrc=wsrc, r0=r0, c0=c0, buf=buf, i=i):
                if i >= 2:
                    e.wait_ge(GC.sem, dq_state["g"][i - 2][0])
                dinc(e.dma_start(out=dq_in[:, buf * 1024:(buf + 1) * 1024],
                                 in_=wsrc[r0:r0 + 128, c0:c0 + 1024]), s_dqld[buf])
            ld = (s_dqld[buf].sem, s_dqld[buf].n)

            @block.vector
            def _(e, buf=buf, ld=ld, i=i):
                dwait(e, ld)
                if i >= 1:
                    e.wait_ge(GC.sem, dq_state["g"][i - 1][1])
                w3 = dq_in[:, buf * 1024:(buf + 1) * 1024].rearrange(
                    "p (b k) -> p b k", k=32)
                e.reduce_max(dq_s[:, :].unsqueeze(2), w3, axis=AXX,
                             apply_absolute_value=True)
                e.tensor_scalar(dq_s[:, :], dq_s[:, :], 1.0 / 7.0, 1e-12,
                                ALU.mult, ALU.max).then_inc(VC.sem, 1)
                VC.n += 1
                e.wait_ge(VC.sem, VC.n)
                e.reciprocal(dq_r[:, :], dq_s[:, :]).then_inc(VC.sem, 1)
            VC.n += 1
            vv = VC.n
            dq_state["v"].append(vv)

            @block.gpsimd
            def _(e, buf=buf, vv=vv, i=i):
                e.wait_ge(VC.sem, vv)
                if i >= 2:
                    dwait(e, dq_state["st"][i - 2])
                w3 = dq_in[:, buf * 1024:(buf + 1) * 1024].rearrange(
                    "p (b k) -> p b k", k=32)
                r3 = dq_r[:, :].unsqueeze(2).broadcast_to([128, 32, 32])
                s3 = dq_s[:, :].unsqueeze(2).broadcast_to([128, 32, 32])
                t3 = dq_t[:, :].rearrange("p (b k) -> p b k", k=32)
                o3 = dq_o[:, buf * 1024:(buf + 1) * 1024].rearrange(
                    "p (b k) -> p b k", k=32)
                e.tensor_tensor(t3, w3, r3, ALU.mult).then_inc(GC.sem, 1)
                e.tensor_scalar(dq_t[:, :], dq_t[:, :], MAGIC, -MAGIC, ALU.add, ALU.add)
                e.tensor_tensor(o3, t3, s3, ALU.mult).then_inc(GC.sem, 1)
            GC.n += 2
            dq_state["g"].append((GC.n - 1, GC.n))

            @block.sync
            def _(e, wdst=wdst, r0=r0, c0=c0, buf=buf, g=GC.n):
                e.wait_ge(GC.sem, g)
                dinc(e.dma_start(out=wdst[r0:r0 + 128, c0:c0 + 1024],
                                 in_=dq_o[:, buf * 1024:(buf + 1) * 1024]), s_dqst[buf])
            dq_state["st"].append((s_dqst[buf].sem, s_dqst[buf].n))

    @block.sync
    def _(e, toks=[(s.sem, s.n) for s in (s_h1sp[0], s_h1sp[1], s_qsp[0], s_qsp[1])]):
        for tk in toks:
            dwait(e, tk)

    av_free_vc = VC.n
    h1s_free = []
    kseq = 0
    bias_done = [0] * NPAIR
    kev_sa = []
    last_exp = []
    epv_seq = []
    x2st_seq = []

    attn_tm_last = []
    last_kbred = []
    for p in range(NPAIR):
        pb = p % 2

        @block.gpsimd
        def _(e, p=p, pb=pb):
            if p >= 2:
                e.wait_ge(TM.sem, kmm_last[p - 2])
            for d in range(8):
                dinc(e.dma_start(out=wks[:, (pb * 8 + d) * 128:(pb * 8 + d + 1) * 128],
                                 in_=wqkv[d * 128:(d + 1) * 128,
                                          1024 + p * 128:1024 + (p + 1) * 128]),
                     g_wk[pb])
        wk_d = (g_wk[pb].sem, g_wk[pb].n)

        @block.sync
        def _(e, p=p, pb=pb):
            if p >= 2:
                e.wait_ge(TM.sem, attn_tm_last[p - 2])
            dinc(e.dma_start(out=QA_sb[:, pb * OWN:(pb + 1) * OWN],
                             in_=q_dram[0:64, p * OWN:(p + 1) * OWN]), s_qpld[pb])
            dinc(e.dma_start(out=QB_sb[:, pb * OWN:(pb + 1) * OWN],
                             in_=q_dram[64:128, p * OWN:(p + 1) * OWN]), s_qpld[pb])
        qld_sd = (s_qpld[pb].sem, s_qpld[pb].n)

        kmm_l = []
        kev_t = []
        for t in range(8):
            tb2 = kseq % 2
            kseq += 1

            @block.sync
            def _(e, t=t, tb2=tb2, ks=kseq - 1):
                if ks >= 2:
                    e.wait_ge(TM.sem, h1s_free[ks - 2])
                for d in range(8):
                    dinc(e.dma_start(
                        out=h1s[:, (tb2 * 8 + d) * 512:(tb2 * 8 + d + 1) * 512],
                        in_=h1f_dram[:, (d * 8 + t) * 512:(d * 8 + t + 1) * 512]),
                        s_h1ld[tb2])
            hld = (s_h1ld[tb2].sem, s_h1ld[tb2].n)

            @block.tensor
            def _(e, p=p, t=t, tb2=tb2, pb=pb, hld=hld, wk=wk_d, af=av_free_vc,
                  kprev=(None if t == 0 else None)):
                dwait(e, hld)
                if t == 0:
                    dwait(e, wk)
                    e.wait_ge(VC.sem, af)
                else:
                    e.wait_ge(SA.sem, kev_t[t - 1])
                for d in range(8):
                    ins = e.matmul(
                        pC[:, :],
                        wks[:, (pb * 8 + d) * 128:(pb * 8 + d + 1) * 128],
                        h1s[:, (tb2 * 8 + d) * 512:(tb2 * 8 + d + 1) * 512],
                        start=(d == 0), stop=(d == 7))
                ins.then_inc(TM.sem, 1)
            TM.n += 1
            kmm_l.append(TM.n)
            h1s_free.append(TM.n)

            @block.scalar
            def _(e, p=p, t=t, pb=pb, w=kmm_l[t]):
                e.wait_ge(TM.sem, w)
                e.copy(K_p[:, pb * S + t * 512:pb * S + (t + 1) * 512],
                       pC[:, :]).then_inc(SA.sem, 1)
            SA.n += 1
            kev_t.append(SA.n)
        kev_sa.append(SA.n)
        if p == 0:
            kmm_last = []
        kmm_last.append(kmm_l[-1])

        ntr_mm, ntr_ev = [], []
        for r in range(64):
            psl = pC if r % 2 == 0 else pD

            @block.tensor
            def _(e, p=p, pb=pb, r=r, psl=psl, ke=kev_sa[p], ge=gd_eshift,
                  kbp=(last_kbred[-1] if last_kbred else 0)):
                if r == 0:
                    e.wait_ge(SA.sem, ke)
                    dwait(e, ge)
                if r <= 1 and kbp:
                    e.wait_ge(VC.sem, kbp)
                if r >= 2:
                    e.wait_ge(VC.sem, ntr_ev[r - 2])
                ps_bf = psl[:, :].bitcast(BF16)
                e.transpose(ps_bf[0:64, 0:128],
                            K_p[:, pb * S + r * 64:pb * S + (r + 1) * 64],
                            ident_sb[:, :]).then_inc(TM.sem, 1)
            TM.n += 1
            ntr_mm.append(TM.n)

            @block.vector
            def _(e, r=r, psl=psl, w=ntr_mm[r]):
                e.wait_ge(TM.sem, w)
                dstA = NKT_A[:, :].rearrange("p (g r) -> p g r", r=64)[:, :, r]
                dstB = NKT_B[:, :].rearrange("p (g r) -> p g r", r=64)[:, :, r]
                ps_bf = psl[:, :].bitcast(BF16)
                e.tensor_copy(dstA, ps_bf[0:64, 0:64])
                e.tensor_copy(dstB, ps_bf[0:64, 64:128]).then_inc(VC.sem, 1)
            VC.n += 1
            ntr_ev.append(VC.n)
        nkt_done = VC.n

        qb_mm, qb_red = [], []
        for sl in range(8):
            qsrc = QA_sb if sl < 4 else QB_sb
            so = (sl % 4) * 512

            @block.vector
            def _(e, pb=pb, sl=sl, qsrc=qsrc, so=so, nd=nkt_done, ql=qld_sd):
                if sl == 0:
                    e.wait_ge(VC.sem, nd)
                    dwait(e, ql)
                    e.memset(macc[:, 0:1], 0.0)
                if sl >= 1:
                    e.wait_ge(TM.sem, qb_mm[sl - 1])
                qq = qsrc[:, pb * OWN + so:pb * OWN + so + 512]
                e.tensor_tensor(ksq_sb[0:64, :], qq, qq, ALU.mult).then_inc(VC.sem, 1)
            VC.n += 1
            qv = VC.n

            @block.tensor
            def _(e, sl=sl, qv=qv, af=av_free_vc):
                e.wait_ge(VC.sem, qv)
                if sl == 0:
                    e.wait_ge(VC.sem, af)
                else:
                    e.wait_ge(VC.sem, qb_red[sl - 1])
                e.matmul(pD[:, :], ones_bf[0:64, :], ksq_sb[0:64, :], start=True,
                         stop=True).then_inc(TM.sem, 1)
            TM.n += 1
            qb_mm.append(TM.n)

            @block.vector
            def _(e, sl=sl, w=qb_mm[sl]):
                e.wait_ge(TM.sem, w)
                e.reduce_max(macc[:, 1:2], pD[:, :], axis=AXX)
                e.tensor_tensor(macc[:, 0:1], macc[:, 0:1], macc[:, 1:2],
                                ALU.max).then_inc(VC.sem, 1)
            VC.n += 1
            qb_red.append(VC.n)

        kb_mm, kb_red = [], []
        for sl in range(16):
            ksrc = NKT_A if sl < 8 else NKT_B
            so = (sl % 8) * 512

            @block.vector
            def _(e, sl=sl, ksrc=ksrc, so=so, qw=qb_red[-1]):
                if sl == 0:
                    e.wait_ge(VC.sem, qw)
                    e.memset(macc[:, 2:3], 0.0)
                if sl >= 1:
                    e.wait_ge(TM.sem, kb_mm[sl - 1])
                kk = ksrc[:, so:so + 512]
                e.tensor_tensor(ksq_sb[0:64, :], kk, kk, ALU.mult).then_inc(VC.sem, 1)
            VC.n += 1
            kv = VC.n

            @block.tensor
            def _(e, sl=sl, kv=kv):
                e.wait_ge(VC.sem, kv)
                if sl >= 1:
                    e.wait_ge(VC.sem, kb_red[sl - 1])
                e.matmul(pD[:, :], ones_bf[0:64, :], ksq_sb[0:64, :], start=True,
                         stop=True).then_inc(TM.sem, 1)
            TM.n += 1
            kb_mm.append(TM.n)

            @block.vector
            def _(e, sl=sl, w=kb_mm[sl]):
                e.wait_ge(TM.sem, w)
                e.reduce_max(macc[:, 3:4], pD[:, :], axis=AXX)
                e.tensor_tensor(macc[:, 2:3], macc[:, 2:3], macc[:, 3:4],
                                ALU.max).then_inc(VC.sem, 1)
            VC.n += 1
            kb_red.append(VC.n)

        last_kbred.append(kb_red[-1])

        @block.vector
        def _(e, p=p, w=kb_red[-1]):
            e.wait_ge(VC.sem, w)
            e.tensor_copy(tiny[0:1, 0:1], macc[0:1, 0:1])
            e.tensor_copy(tiny[0:1, 1:2], macc[0:1, 2:3])
            e.drain()
            e.tensor_tensor(tiny[0:1, 2:3], tiny[0:1, 0:1], tiny[0:1, 1:2],
                            ALU.mult)
            if p == 0:
                e.drain()
                e.tensor_copy(dbg_sb[0:1, 0:3], tiny[0:1, 0:3])
            e.drain().then_inc(VC.sem, 1)
        VC.n += 1
        bv = VC.n

        @block.scalar
        def _(e, bv=bv):
            e.wait_ge(VC.sem, bv)
            e.activation(tiny[0:1, 2:3], tiny[0:1, 2:3], AF.Sqrt).then_inc(SA.sem, 1)
        SA.n += 1
        bsq = SA.n

        @block.vector
        def _(e, p=p, bsq=bsq):
            e.wait_ge(SA.sem, bsq)
            if p == 0:
                e.tensor_copy(dbg_sb[0:1, 3:4], tiny[0:1, 2:3])
                e.drain()
            e.tensor_single_scalar(tiny[0:1, 2:3], tiny[0:1, 2:3], -0.125,
                                   ALU.mult)
            if p == 0:
                e.drain()
                e.tensor_copy(dbg_sb[0:1, 4:5], tiny[0:1, 2:3])
            e.drain().then_inc(VC.sem, 1)
        VC.n += 1
        bneg = VC.n

        @block.sync
        def _(e, bneg=bneg, p=p, pb=pb):
            e.wait_ge(VC.sem, bneg)
            if p >= 2:
                e.wait_ge(SA.sem, last_exp[p - 2])
            dinc(e.dma_start(out=scal_dram[0:1, 15:16], in_=tiny[0:1, 2:3]), s_bias[pb])
        b1 = (s_bias[pb].sem, s_bias[pb].n)

        @block.sync
        def _(e, b1=b1, p=p, pb=pb):
            dwait(e, b1)
            dinc(e.dma_start(out=bias_sb[:, pb:pb + 1],
                             in_=scal_dram[0:1, 15:16].partition_broadcast(128)),
                 s_bias[pb])
        bias_done[p] = (s_bias[pb].sem, s_bias[pb].n)

        for u in range(4):
            nch = 8 * (u + 1)
            qkt_d, mask_d, exp_d, av_d = {}, {}, {}, {}

            iseq = p * 4 + u

            @block.sync
            def _(e, p=p, u=u, iseq=iseq):
                if iseq >= 2:
                    e.wait_ge(VC.sem, epv_seq[iseq - 2])
                dinc(e.dma_start(out=xot[:, (u % 2) * 512:(u % 2 + 1) * 512],
                                 in_=xoT[p * 128:(p + 1) * 128, u * 512:(u + 1) * 512]),
                     s_xot[u % 2])
            xot_ld = (s_xot[u % 2].sem, s_xot[u % 2].n)

            for c in range(nch):
                ring = ring0 if c % 2 == 0 else ring1
                diag = c >= 8 * u
                qoff = 64 * (c - 8 * u) if diag else 0

                @block.tensor
                def _(e, p=p, pb=pb, u=u, c=c, ring=ring, qoff=qoff, af=av_free_vc):
                    if c >= 2:
                        e.wait_ge(SA.sem, exp_d[c - 2])
                    if c == 0:
                        e.wait_ge(VC.sem, af)
                    kA = NKT_A[:, c * 128:(c + 1) * 128]
                    kB = NKT_B[:, c * 128:(c + 1) * 128]
                    qA = QA_sb[:, pb * OWN + u * 512 + qoff:pb * OWN + (u + 1) * 512]
                    qB = QB_sb[:, pb * OWN + u * 512 + qoff:pb * OWN + (u + 1) * 512]
                    e.matmul(ring[:, qoff:512], kA, qA, start=True, stop=True)
                    e.matmul(ring[:, 512 + qoff:1024], kB, qB, start=True,
                             stop=True).then_inc(TM.sem, 1)
                TM.n += 1
                qkt_d[c] = TM.n

                if diag:
                    @block.vector
                    def _(e, c=c, ring=ring, qoff=qoff, w=qkt_d[c]):
                        e.wait_ge(TM.sem, w)
                        e.tensor_tensor(ring[:, qoff:qoff + 64], ring[:, qoff:qoff + 64],
                                        dmask_sb[:, :], ALU.add)
                        e.tensor_tensor(ring[:, 512 + qoff:512 + qoff + 64],
                                        ring[:, 512 + qoff:512 + qoff + 64],
                                        dmask_sb[:, :], ALU.add).then_inc(VC.sem, 1)
                    VC.n += 1
                    mask_d[c] = VC.n

                @block.scalar
                def _(e, p=p, pb=pb, c=c, ring=ring, diag=diag,
                      wq=qkt_d[c], wm=mask_d.get(c), wb=bias_done[p]):
                    if diag:
                        e.wait_ge(VC.sem, wm)
                    else:
                        e.wait_ge(TM.sem, wq)
                    if c == 0:
                        dwait(e, wb)
                    if c >= 3:
                        e.wait_ge(TM.sem, av_d[c - 3])
                    e.activation(P_sb[:, (c % 3) * 1024:(c % 3 + 1) * 1024], ring[:, :],
                                 AF.Exp, bias=bias_sb[:, pb:pb + 1],
                                 scale=0.125).then_inc(SA.sem, 1)
                SA.n += 1
                exp_d[c] = SA.n

                @block.tensor
                def _(e, p=p, c=c, qoff=qoff, nch=nch, w=exp_d[c]):
                    e.wait_ge(SA.sem, w)
                    base = c * 1040 + p * 130
                    lA = V_all[:, base:base + 65]
                    lB = V_all[:, base + 65:base + 130]
                    pa = P_sb[:, (c % 3) * 1024 + qoff:(c % 3) * 1024 + 512]
                    pbb = P_sb[:, (c % 3) * 1024 + 512 + qoff:(c % 3) * 1024 + 1024]
                    e.matmul(pA[0:65, qoff:512], lA, pa, start=(c == 0),
                             stop=(c == nch - 1))
                    e.matmul(pB[0:65, qoff:512], lB, pbb, start=(c == 0),
                             stop=(c == nch - 1)).then_inc(TM.sem, 1)
                TM.n += 1
                av_d[c] = TM.n

            @block.vector
            def _(e, p=p, u=u, w=av_d[nch - 1]):
                e.wait_ge(TM.sem, w)
                e.reciprocal(recA[0:1, :], pA[64:65, 0:512]).then_inc(VC.sem, 1)
                e.reciprocal(recB[0:1, :], pB[64:65, 0:512]).then_inc(VC.sem, 1)
                if p == 0 and u == 0:
                    e.drain()
                    e.tensor_copy(dbg_sb[0:1, 8:16], recA[0:1, 0:8])
                    e.tensor_copy(dbg_sb[0:1, 16:24], recB[0:1, 0:8])
            VC.n += 2
            rec_v = VC.n

            @block.tensor
            def _(e, rv=rec_v):
                e.wait_ge(VC.sem, rv)
                e.matmul(pC[0:64, :], ones_f[0:1, :], recA[0:1, :], start=True, stop=True)
                e.matmul(pD[0:64, :], ones_f[0:1, :], recB[0:1, :], start=True,
                         stop=True).then_inc(TM.sem, 1)
            TM.n += 1
            bc_t = TM.n

            @block.vector
            def _(e, p=p, u=u, bt=bc_t, xl=xot_ld, iseq=iseq):
                e.wait_ge(TM.sem, bt)
                dwait(e, xl)
                if iseq >= 2:
                    dwait(e, x2st_seq[iseq - 2])
                ub = u % 2
                e.tensor_copy(recA[:, :], pC[0:64, :])
                e.tensor_copy(recB[:, :], pD[0:64, :])
                if p == 0 and u == 0:
                    e.tensor_copy(dbg_sb[0:1, 24:32], P_sb[0:1, 1024:1032])
                    e.tensor_copy(dbg_sb[0:1, 32:40], pA[0:1, 0:8])
                    e.tensor_copy(dbg_sb[0:1, 40:48], QA_sb[0:1, 0:8])
                    e.tensor_copy(dbg_sb[0:1, 48:56], NKT_A[0:1, 0:8])
                e.scalar_tensor_tensor(u1s[:, :], pA[0:64, 0:512], ga_sb[0:64, p:p + 1],
                                       recA[:, :], ALU.mult, ALU.mult)
                e.tensor_tensor(x2t[0:64, ub * 512:(ub + 1) * 512], u1s[:, :],
                                xot[0:64, ub * 512:(ub + 1) * 512], ALU.add)
                e.scalar_tensor_tensor(u2s[:, :], pB[0:64, 0:512], gaup_sb[:, p:p + 1],
                                       recB[:, :], ALU.mult, ALU.mult).then_inc(VC.sem, 1)
            VC.n += 1
            ep_v = VC.n

            @block.tensor
            def _(e, ev=ep_v, ge=gd_eshift):
                e.wait_ge(VC.sem, ev)
                dwait(e, ge)
                e.matmul(ring0[:, 0:512], eshift_sb[:, :], u2s[:, :], start=True,
                         stop=True).then_inc(TM.sem, 1)
            TM.n += 1
            rel_t = TM.n

            @block.vector
            def _(e, p=p, u=u, rt=rel_t):
                e.wait_ge(TM.sem, rt)
                ub = u % 2
                e.tensor_tensor(x2t[64:128, ub * 512:(ub + 1) * 512], ring0[64:128, 0:512],
                                xot[64:128, ub * 512:(ub + 1) * 512],
                                ALU.add).then_inc(VC.sem, 1)
            VC.n += 1
            x2_v = VC.n
            av_free_vc = x2_v

            epv_seq.append(x2_v)

            @block.sync
            def _(e, p=p, u=u, xv=x2_v):
                e.wait_ge(VC.sem, xv)
                ub = u % 2
                dinc(e.dma_start(
                    out=x2_dram[:, p * OWN + u * 512:p * OWN + (u + 1) * 512],
                    in_=x2t[:, ub * 512:(ub + 1) * 512]), s_x2st[u % 2])
            x2st_seq.append((s_x2st[u % 2].sem, s_x2st[u % 2].n))

            emit_dequant(3)
        attn_tm_last.append(TM.n)
        last_exp.append(SA.n)
        emit_dequant(2)

    emit_dequant(len(dq_plan))
    dq_last_st = [(s.sem, s.n) for s in (s_dqst[0], s_dqst[1])]
    assert dq_state["i"] == len(dq_plan)

    p2.close()
    s12.close()
    if _PHASES < 3:
        return finish_dummy((s01,))
    barrier()

    s46 = ExitStack()
    gate_sb = s46.enter_context(nc.sbuf_tensor("gate_sb", [128, NCC * OWN], BF16))

    s34 = ExitStack()
    xq_sb = s34.enter_context(nc.sbuf_tensor("xq_sb", [128, 8 * OWN], BF16))

    p3 = ExitStack()
    h2_sb = p3.enter_context(nc.sbuf_tensor("h2_sb", [128, 8 * OWN], BF16))
    xqt_f = p3.enter_context(nc.sbuf_tensor("xqt_f", [128, OWN], F32))
    x2b = p3.enter_context(nc.sbuf_tensor("x2b", [128, 2 * 8 * 512], BF16))
    x2q = p3.enter_context(nc.sbuf_tensor("x2q", [128, 8 * 512], BF16))
    mrep2 = p3.enter_context(nc.sbuf_tensor("mrep2", [128, 512], F32))
    rrep2 = p3.enter_context(nc.sbuf_tensor("rrep2", [128, 512], F32))

    @block.sync
    def _(e, toks=[(s.sem, s.n) for s in (s_x2st[0], s_x2st[1])]):
        for tk in toks:
            dwait(e, tk)

    x2ld_l, st2_l, rst2_l, h2_l, x2q_l = [], [], [], [], []
    for t in range(4):
        tb = t % 2

        @block.sync
        def _(e, t=t, tb=tb):
            if t >= 2:
                e.wait_ge(VC.sem, h2_l[t - 2])
            for d in range(8):
                dinc(e.dma_start(
                    out=x2b[:, (tb * 8 + d) * 512:(tb * 8 + d + 1) * 512],
                    in_=x2_dram[:, d * OWN + t * 512:d * OWN + (t + 1) * 512]),
                    s_x2ld[tb])
        x2ld_l.append((s_x2ld[tb].sem, s_x2ld[tb].n))

        @block.vector
        def _(e, t=t, tb=tb, w=x2ld_l[t]):
            dwait(e, w)
            if t >= 1:
                e.wait_ge(TM.sem, st2_l[t - 1])
            for d in range(8):
                xb = x2b[:, (tb * 8 + d) * 512:(tb * 8 + d + 1) * 512]
                ins = e.tensor_tensor(x2q[:, d * 512:(d + 1) * 512], xb, xb, ALU.mult)
            ins.then_inc(VC.sem, 1)
        VC.n += 1
        x2q_l.append(VC.n)

        @block.tensor
        def _(e, t=t, tb=tb, w=x2q_l[t]):
            e.wait_ge(VC.sem, w)
            if t >= 1:
                e.wait_ge(VC.sem, rst2_l[t - 1])
            for d in range(8):
                xb = x2b[:, (tb * 8 + d) * 512:(tb * 8 + d + 1) * 512]
                e.matmul(pA[:, :], ones_bf[:, :], xb, start=(d == 0), stop=(d == 7))
            for d in range(8):
                ins = e.matmul(pB[:, :], ones_bf[:, :], x2q[:, d * 512:(d + 1) * 512],
                               start=(d == 0), stop=(d == 7))
            ins.then_inc(TM.sem, 1)
        TM.n += 1
        st2_l.append(TM.n)

        @block.vector
        def _(e, t=t, w=st2_l[t]):
            e.wait_ge(TM.sem, w)
            e.tensor_single_scalar(mrep2[:, :], pA[:, :], 1.0 / 1024.0, ALU.mult)
            e.tensor_tensor(rrep2[:, :], mrep2[:, :], mrep2[:, :], ALU.mult)
            e.scalar_tensor_tensor(rrep2[:, :], pB[:, :], 1.0 / 1024.0, rrep2[:, :],
                                   ALU.mult, ALU.subtract).then_inc(VC.sem, 1)
        VC.n += 1
        v2 = VC.n

        @block.scalar
        def _(e, w=v2):
            e.wait_ge(VC.sem, w)
            e.activation(rrep2[:, :], rrep2[:, :], AF.Sqrt, bias=eps_sb[:, 0:1]).then_inc(SA.sem, 1)
        SA.n += 1
        s2 = SA.n

        @block.vector
        def _(e, w=s2):
            e.wait_ge(SA.sem, w)
            e.reciprocal(rrep2[:, :], rrep2[:, :]).then_inc(VC.sem, 1)
            VC.n += 1
            e.wait_ge(VC.sem, VC.n)
        rst2_l.append(VC.n)

        @block.vector
        def _(e, t=t, tb=tb):
            if t == 0:
                e.memset(macc[:, 4:6], 0.0)
            for d in range(8):
                xb = x2b[:, (tb * 8 + d) * 512:(tb * 8 + d + 1) * 512]
                h2 = h2_sb[:, d * OWN + t * 512:d * OWN + (t + 1) * 512]
                e.tensor_tensor(h2, xb, mrep2[:, :], ALU.subtract)
                e.tensor_tensor(h2, h2, rrep2[:, :], ALU.mult)
                e.tensor_scalar(h2, h2, cb1_sb[:, d:d + 1], sbv_sb[:, d:d + 1],
                                ALU.mult, ALU.add)
                e.reduce_max(macc[:, 5:6], h2, axis=AXX, apply_absolute_value=True)
                ins = e.tensor_tensor(macc[:, 4:5], macc[:, 4:5], macc[:, 5:6], ALU.max)
            ins.then_inc(VC.sem, 1)
        VC.n += 1
        h2_l.append(VC.n)

    pm0 = part_max(4, 4, h2_l[-1])
    ar1 = allreduce_max(0, 4, 1, pm0)

    @block.vector
    def _(e, w=ar1):
        dwait(e, w)
        e.tensor_scalar(tiny[0:1, 5:6], tiny[0:1, 4:5], 1.0 / QMAX, 1e-12,
                        ALU.mult, ALU.max)
        recip1(e, tiny[0:1, 6:7], tiny[0:1, 5:6]).then_inc(VC.sem, 1)
    VC.n += 1
    s0_v = VC.n
    s0_sd = bcast_scalars(5, 2, 0, s0_v)

    @block.vector
    def _(e, w=s0_sd):
        dwait(e, w)
        for d in range(8):
            h2 = h2_sb[:, d * OWN:(d + 1) * OWN]
            xq = xq_sb[:, d * OWN:(d + 1) * OWN]
            e.tensor_scalar(xqt_f[:, :], h2, scv_sb[:, 1:2], MAGIC, ALU.mult, ALU.add)
            ins = e.tensor_scalar(xq, xqt_f[:, :], -MAGIC, scv_sb[:, 0:1],
                                  ALU.add, ALU.mult)
        ins.then_inc(VC.sem, 1)
    VC.n += 1

    p3.close()
    if _PHASES < 4:
        return finish_dummy((s34, s46))
    barrier()

    p4 = ExitStack()
    wtile = p4.enter_context(nc.sbuf_tensor("wtile", [128, 2 * 8 * 128], BF16))
    gtile = p4.enter_context(nc.sbuf_tensor("gtile", [128, 2 * 8 * 128], BF16))
    upt = p4.enter_context(nc.sbuf_tensor("upt", [128, 2 * OWN], BF16))

    @block.vector
    def _(e):
        e.memset(macc[:, 0:4], NEG_PAD)
        e.memset(macc[:, 6:8], NEG_PAD).then_inc(VC.sem, 1)
    VC.n += 1
    macc_z = VC.n

    upmm_l, gmm_l, upev_l, gev_l, upst_l, upmax_l = [], [], [], [], [], []
    ups = (pA[:, :], pB[:, :], pC[:, :], pD[:, :])
    gps = (ring0[:, 0:512], ring0[:, 512:1024], ring1[:, 0:512], ring1[:, 512:1024])

    for cc in range(NCC):
        b = cc % 2

        @block.scalar
        def _(e, cc=cc, b=b, dqw=dq_last_st):
            if cc == 0:
                for tk in dqw:
                    dwait(e, tk)
            if cc >= 2:
                e.wait_ge(TM.sem, upmm_l[cc - 2])
                e.wait_ge(TM.sem, gmm_l[cc - 2])
            for d in range(8):
                dinc(e.dma_start(
                    out=wtile[:, (b * 8 + d) * 128:(b * 8 + d + 1) * 128],
                    in_=wupq_dram[cc * 128:(cc + 1) * 128, d * 128:(d + 1) * 128],
                    transpose=True), c_w[b])
                dinc(e.dma_start(
                    out=gtile[:, (b * 8 + d) * 128:(b * 8 + d + 1) * 128],
                    in_=wgq_dram[cc * 128:(cc + 1) * 128, d * 128:(d + 1) * 128],
                    transpose=True), c_w[b])
        wt_d = (c_w[b].sem, c_w[b].n)

        @block.tensor
        def _(e, cc=cc, b=b, wd=wt_d):
            dwait(e, wd)
            if cc >= 1:
                e.wait_ge(SA.sem, upev_l[cc - 1])
            for d in range(8):
                wt = wtile[:, (b * 8 + d) * 128:(b * 8 + d + 1) * 128]
                for tt in range(4):
                    ins = e.matmul(
                        ups[tt], wt,
                        xq_sb[:, d * OWN + tt * 512:d * OWN + (tt + 1) * 512],
                        start=(d == 0), stop=(d == 7))
            ins.then_inc(TM.sem, 1)
        TM.n += 1
        upmm_l.append(TM.n)

        @block.scalar
        def _(e, cc=cc, b=b, w=upmm_l[cc]):
            e.wait_ge(TM.sem, w)
            if cc >= 2:
                dwait(e, upst_l[cc - 2])
                e.wait_ge(VC.sem, upmax_l[cc - 2])
            for tt in range(4):
                ins = e.copy(upt[:, b * OWN + tt * 512:b * OWN + (tt + 1) * 512],
                             ups[tt])
            ins.then_inc(SA.sem, 1)
        SA.n += 1
        upev_l.append(SA.n)

        @block.sync
        def _(e, cc=cc, b=b, w=upev_l[cc]):
            e.wait_ge(SA.sem, w)
            dinc(e.dma_start(out=upf_dram[cc * 128:(cc + 1) * 128, :],
                             in_=upt[:, b * OWN:(b + 1) * OWN]), s_upst[b])
        upst_l.append((s_upst[b].sem, s_upst[b].n))

        @block.vector
        def _(e, cc=cc, b=b, w=upev_l[cc], mz=macc_z):
            e.wait_ge(SA.sem, w)
            if cc == 0:
                e.wait_ge(VC.sem, mz)
            e.reduce_max(macc[:, 1:2], upt[:, b * OWN:(b + 1) * OWN], axis=AXX,
                         apply_absolute_value=True)
            e.tensor_tensor(macc[:, 0:1], macc[:, 0:1], macc[:, 1:2],
                            ALU.max).then_inc(VC.sem, 1)
        VC.n += 1
        upmax_l.append(VC.n)

        @block.tensor
        def _(e, cc=cc, b=b):
            if cc >= 1:
                e.wait_ge(VC.sem, gev_l[cc - 1])
            for d in range(8):
                gt = gtile[:, (b * 8 + d) * 128:(b * 8 + d + 1) * 128]
                for tt in range(4):
                    ins = e.matmul(
                        gps[tt], gt,
                        xq_sb[:, d * OWN + tt * 512:d * OWN + (tt + 1) * 512],
                        start=(d == 0), stop=(d == 7))
            ins.then_inc(TM.sem, 1)
        TM.n += 1
        gmm_l.append(TM.n)

        @block.vector
        def _(e, cc=cc, w=gmm_l[cc]):
            e.wait_ge(TM.sem, w)
            for tt in range(4):
                e.tensor_copy(gate_sb[:, cc * OWN + tt * 512:cc * OWN + (tt + 1) * 512],
                              gps[tt])
            gg = gate_sb[:, cc * OWN:(cc + 1) * OWN]
            e.reduce_max(macc[:, 3:4], gg, axis=AXX, apply_absolute_value=True)
            e.tensor_tensor(macc[:, 2:3], macc[:, 2:3], macc[:, 3:4], ALU.max)
            e.reduce_max(macc[:, 7:8], gg, axis=AXX)
            e.tensor_tensor(macc[:, 6:7], macc[:, 6:7], macc[:, 7:8],
                            ALU.max).then_inc(VC.sem, 1)
        VC.n += 1
        gev_l.append(VC.n)

    pm1 = part_max(0, 8, gev_l[-1])
    pm2 = part_max(2, 9, pm1)
    pm3 = part_max(6, 10, pm2)
    ar2 = allreduce_max(1, 8, 3, pm3)

    @block.vector
    def _(e, w=ar2):
        dwait(e, w)
        e.tensor_scalar(tiny[0:1, 12:13], tiny[0:1, 9:10], 1.0 / QMAX, 1e-12,
                        ALU.mult, ALU.max)
        recip1(e, tiny[0:1, 11:12], tiny[0:1, 12:13])
        e.tensor_scalar(tiny[0:1, 3:4], tiny[0:1, 8:9], 1.0 / QMAX, 1e-12,
                        ALU.mult, ALU.max)
        recip1(e, tiny[0:1, 14:15], tiny[0:1, 3:4])
        e.tensor_tensor(tiny[0:1, 0:1], tiny[0:1, 10:11], tiny[0:1, 11:12], ALU.mult)
        e.drain()
        e.tensor_scalar(tiny[0:1, 0:1], tiny[0:1, 0:1], MAGIC, -MAGIC,
                        ALU.add, ALU.add).then_inc(VC.sem, 1)
    VC.n += 1
    m3a = VC.n

    @block.scalar
    def _(e, w=m3a):
        e.wait_ge(VC.sem, w)
        e.activation(tiny[0:1, 1:2], tiny[0:1, 0:1], AF.Sigmoid,
                     scale=tiny[0:1, 12:13]).then_inc(SA.sem, 1)
    SA.n += 1
    m3b = SA.n

    @block.vector
    def _(e, w=m3b):
        e.wait_ge(SA.sem, w)
        e.tensor_scalar(tiny[0:1, 1:2], tiny[0:1, 1:2], 65536.0, MAGIC, ALU.mult, ALU.add)
        e.drain()
        e.tensor_scalar(tiny[0:1, 1:2], tiny[0:1, 1:2], -MAGIC, 65535.0, ALU.add, ALU.min)
        e.drain()
        e.tensor_tensor(tiny[0:1, 2:3], tiny[0:1, 0:1], tiny[0:1, 1:2], ALU.mult)
        e.drain()
        e.tensor_tensor(tiny[0:1, 2:3], tiny[0:1, 2:3], tiny[0:1, 12:13], ALU.mult)
        e.drain()
        e.tensor_single_scalar(tiny[0:1, 2:3], tiny[0:1, 2:3], 1.0 / 65536.0, ALU.mult)
        e.drain()
        e.tensor_scalar(tiny[0:1, 2:3], tiny[0:1, 2:3], 1.0 / QMAX, 1e-12,
                        ALU.mult, ALU.max)
        recip1(e, tiny[0:1, 13:14], tiny[0:1, 2:3])
        e.drain()
        e.tensor_tensor(tiny[0:1, 13:14], tiny[0:1, 13:14], tiny[0:1, 12:13], ALU.mult)
        e.drain()
        e.tensor_single_scalar(tiny[0:1, 13:14], tiny[0:1, 13:14], 1.0 / 65536.0,
                               ALU.mult).then_inc(VC.sem, 1)
    VC.n += 1
    m3c = VC.n
    bc2 = bcast_scalars(11, 4, 2, m3c)

    p4.close()
    s34.close()
    if _PHASES < 5:
        return finish_dummy((s46,))
    barrier()

    p5 = ExitStack()
    wdnT = p5.enter_context(nc.sbuf_tensor("wdnT", [128, NCC * 8 * 128], BF16))
    upin = p5.enter_context(nc.sbuf_tensor("upin", [128, 2 * OWN], BF16))
    scr1 = p5.enter_context(nc.sbuf_tensor("scr1", [128, OWN], F32))
    scr2 = p5.enter_context(nc.sbuf_tensor("scr2", [128, OWN], F32))
    sbI = p5.enter_context(nc.sbuf_tensor("sbI", [128, 2 * OWN], BF16))
    sbS = p5.enter_context(nc.sbuf_tensor("sbS", [128, 2 * OWN], BF16))
    sbU = p5.enter_context(nc.sbuf_tensor("sbU", [128, 2 * OWN], BF16))
    vtb = p5.enter_context(nc.sbuf_tensor("vtb", [128, OWN], BF16))

    @block.vector
    def _(e):
        e.memset(macc[:, 0:2], NEG_PAD).then_inc(VC.sem, 1)
    VC.n += 1
    macc5_z = VC.n

    @block.sync
    def _(e, toks=[(s.sem, s.n) for s in (s_upst[0], s_upst[1])]):
        for tk in toks:
            dwait(e, tk)

    upld_l, s5_l, v5_l, u5_l = [], [], [], []
    for cc in range(NCC):
        b = cc % 2

        @block.sync
        def _(e, cc=cc, b=b):
            if cc >= 2:
                e.wait_ge(SA.sem, u5_l[cc - 2])
            dinc(e.dma_start(out=upin[:, b * OWN:(b + 1) * OWN],
                             in_=upf_dram[cc * 128:(cc + 1) * 128, :]), s_upld[b])
        upld_l.append((s_upld[b].sem, s_upld[b].n))

        @block.scalar
        def _(e, cc=cc, bcw=bc2):
            if cc == 0:
                dwait(e, bcw)
            for d in range(8):
                dinc(e.dma_start(
                    out=wdnT[:, (cc * 8 + d) * 128:(cc * 8 + d + 1) * 128],
                    in_=wdnq_dram[d * 128:(d + 1) * 128, cc * 128:(cc + 1) * 128],
                    transpose=True), c_wd)

        @block.scalar
        def _(e, cc=cc, b=b, w=upld_l[cc]):
            if cc >= 2:
                e.wait_ge(VC.sem, v5_l[cc - 2])
            gg = gate_sb[:, cc * OWN:(cc + 1) * OWN]
            gI = sbI[:, b * OWN:(b + 1) * OWN]
            gS = sbS[:, b * OWN:(b + 1) * OWN]
            gU = sbU[:, b * OWN:(b + 1) * OWN]
            e.activation(scr1[:, :], gg, AF.Identity, scale=scv_sb[:, 2:3],
                         bias=mgc_sb[:, 0:1])
            e.add(gI, scr1[:, :], nmg_sb[:, 0:1])
            e.activation(gS, gI, AF.Sigmoid,
                         scale=scv_sb[:, 3:4]).then_inc(SA.sem, 1)
            dwait(e, w)
            e.activation(scr1[:, :], upin[:, b * OWN:(b + 1) * OWN], AF.Identity,
                         scale=scv_sb[:, 5:6], bias=mgc_sb[:, 0:1])
            e.add(gU, scr1[:, :], nmg_sb[:, 0:1]).then_inc(SA.sem, 1)
        SA.n += 2
        s5_l.append(SA.n - 1)
        u5_l.append(SA.n)

        @block.vector
        def _(e, cc=cc, b=b, ws=s5_l[cc], wu=u5_l[cc], mz=macc5_z):
            gI = sbI[:, b * OWN:(b + 1) * OWN]
            gS = sbS[:, b * OWN:(b + 1) * OWN]
            gU = sbU[:, b * OWN:(b + 1) * OWN]
            e.wait_ge(SA.sem, ws)
            if cc == 0:
                e.wait_ge(VC.sem, mz)
            e.tensor_scalar(scr2[:, :], gS, 65536.0, MAGIC, ALU.mult, ALU.add)
            e.tensor_scalar(vtb[:, :], scr2[:, :], -MAGIC, 65535.0, ALU.add, ALU.min)
            e.tensor_tensor(gI, gI, vtb[:, :], ALU.mult)
            e.tensor_scalar(scr2[:, :], gI, scv_sb[:, 4:5], MAGIC,
                            ALU.mult, ALU.add)
            e.wait_ge(SA.sem, wu)
            gg = gate_sb[:, cc * OWN:(cc + 1) * OWN]
            e.scalar_tensor_tensor(gg, scr2[:, :], -MAGIC, gU,
                                   ALU.add, ALU.mult)
            e.reduce_max(macc[:, 1:2], gg, axis=AXX, apply_absolute_value=True)
            e.tensor_tensor(macc[:, 0:1], macc[:, 0:1], macc[:, 1:2],
                            ALU.max).then_inc(VC.sem, 1)
        VC.n += 1
        v5_l.append(VC.n)

    pm4 = part_max(0, 0, v5_l[-1])
    ar3 = allreduce_max(2, 0, 1, pm4)

    @block.vector
    def _(e, w=ar3):
        dwait(e, w)
        e.tensor_tensor(tiny[0:1, 1:2], tiny[0:1, 2:3], tiny[0:1, 3:4], ALU.mult)
        e.drain()
        e.tensor_tensor(tiny[0:1, 5:6], tiny[0:1, 0:1], tiny[0:1, 1:2], ALU.mult)
        e.drain()
        e.tensor_scalar(tiny[0:1, 7:8], tiny[0:1, 5:6], 1.0 / QMAX, 1e-12,
                        ALU.mult, ALU.max)
        recip1(e, tiny[0:1, 6:7], tiny[0:1, 7:8])
        e.drain()
        e.tensor_tensor(tiny[0:1, 6:7], tiny[0:1, 6:7], tiny[0:1, 1:2],
                        ALU.mult).then_inc(VC.sem, 1)
    VC.n += 1
    s4_v = VC.n
    bc3 = bcast_scalars(6, 2, 6, s4_v)

    @block.vector
    def _(e, w=bc3):
        dwait(e, w)
        e.tensor_tensor(gbs4_sb[:, :], gb_sb[:, :],
                        scv_sb[:, 7:8].broadcast_to([128, 8]), ALU.mult).then_inc(VC.sem, 1)
    VC.n += 1
    gbs4_v = VC.n

    oi_l = []
    for cc in range(NCC):
        @block.scalar
        def _(e, cc=cc, gw=gbs4_v):
            if cc == 0:
                e.wait_ge(VC.sem, gw)
            gg = gate_sb[:, cc * OWN:(cc + 1) * OWN]
            e.activation(scr1[:, :], gg, AF.Identity, scale=scv_sb[:, 6:7],
                         bias=mgc_sb[:, 0:1])
            e.add(gg, scr1[:, :], nmg_sb[:, 0:1]).then_inc(SA.sem, 1)
        SA.n += 1
        oi_l.append(SA.n)

    wdnT_all = (c_wd.sem, c_wd.n)
    if _PHASES < 6:
        return finish_dummy((p5, s46))
    p5_close_after = oi_l[-1]
    barrier()

    p6 = ExitStack()
    x2d = p6.enter_context(nc.sbuf_tensor("x2d", [128, 2 * 512], BF16))
    outt = p6.enter_context(nc.sbuf_tensor("outt", [128, 2 * 512], F32))

    dn_l, ep_l, st_l, x2l_l = [], [], [], []
    for Dc in range(8):
        psD = (pA[:, :], pB[:, :], pC[:, :], pD[:, :]) if Dc % 2 == 0 else \
            (ring0[:, 0:512], ring0[:, 512:1024], ring1[:, 0:512], ring1[:, 512:1024])

        @block.tensor
        def _(e, Dc=Dc, psD=psD, ww=wdnT_all, ow=oi_l[-1]):
            if Dc == 0:
                dwait(e, ww)
                e.wait_ge(SA.sem, ow)
            if Dc >= 2:
                e.wait_ge(VC.sem, ep_l[4 * (Dc - 2) + 3])
            for cc in range(NCC):
                wt = wdnT[:, (cc * 8 + Dc) * 128:(cc * 8 + Dc + 1) * 128]
                for tt in range(4):
                    ins = e.matmul(
                        psD[tt], wt,
                        gate_sb[:, cc * OWN + tt * 512:cc * OWN + (tt + 1) * 512],
                        start=(cc == 0), stop=(cc == NCC - 1))
            ins.then_inc(TM.sem, 1)
        TM.n += 1
        dn_l.append(TM.n)

        for tt in range(4):
            ib = (Dc * 4 + tt) % 2

            @block.sync
            def _(e, Dc=Dc, tt=tt, ib=ib):
                if Dc * 4 + tt >= 2:
                    e.wait_ge(VC.sem, ep_l[Dc * 4 + tt - 2])
                dinc(e.dma_start(
                    out=x2d[:, ib * 512:(ib + 1) * 512],
                    in_=x2_dram[:, Dc * OWN + tt * 512:Dc * OWN + (tt + 1) * 512]),
                    s_x2d[ib])
            x2l_l.append((s_x2d[ib].sem, s_x2d[ib].n))

            @block.vector
            def _(e, Dc=Dc, tt=tt, ib=ib, psD=psD, xl=x2l_l[-1], dw=dn_l[Dc]):
                e.wait_ge(TM.sem, dw)
                dwait(e, xl)
                if Dc * 4 + tt >= 2:
                    dwait(e, st_l[Dc * 4 + tt - 2])
                e.scalar_tensor_tensor(outt[:, ib * 512:(ib + 1) * 512], psD[tt],
                                       gbs4_sb[:, Dc:Dc + 1],
                                       x2d[:, ib * 512:(ib + 1) * 512],
                                       ALU.mult, ALU.add).then_inc(VC.sem, 1)
            VC.n += 1
            ep_l.append(VC.n)

            @block.sync
            def _(e, Dc=Dc, tt=tt, ib=ib, w=ep_l[-1]):
                e.wait_ge(VC.sem, w)
                dinc(e.dma_start(
                    out=out[Dc * 128:(Dc + 1) * 128, tt * 512:(tt + 1) * 512],
                    in_=outt[:, ib * 512:(ib + 1) * 512]), s_out[ib])
            st_l.append((s_out[ib].sem, s_out[ib].n))

    if os.environ.get("K_DUMP", "") == "scv":
        @block.vector
        def _(e):
            e.tensor_copy(outt[:, 0:16], scv_sb[:, :])
            e.tensor_copy(outt[0:1, 16:32], tiny[0:1, :])
            e.tensor_copy(outt[0:1, 32:96], dbg_sb[0:1, :]).then_inc(VC.sem, 1)
        VC.n += 1

        @block.sync
        def _(e, wv=VC.n, tok0=(s_out[0].sem, s_out[0].n), tok1=(s_out[1].sem, s_out[1].n)):
            dwait(e, tok0)
            dwait(e, tok1)
            e.wait_ge(VC.sem, wv)
            dinc(e.dma_start(out=out[0:128, 0:96], in_=outt[:, 0:96]), s_out[0])

    @block.sync
    def _(e, toks=[(s.sem, s.n) for s in (s_out[0], s_out[1])]):
        for tk in toks:
            dwait(e, tk)

    p6.close()
    p5.close()
    s46.close()
    es.close()
    return nc


_NC_CACHE = {}


def _get_nc():
    if "nc" not in _NC_CACHE:
        _NC_CACHE["nc"] = build_nc()
    return _NC_CACHE["nc"]


def kernel(L0_attn_x, L0_attn_qkv_weights, L0_ffn_w_gate, L0_ffn_w_up,
           L0_ffn_w_down, L0_c, L0_wmod):
    x = np.ascontiguousarray(np.asarray(L0_attn_x, dtype=np.float32))
    nc = _get_nc()

    eshift_np = np.zeros((64, 128), dtype=np.float32)
    eshift_np[np.arange(64), 64 + np.arange(64)] = 1.0


    wq = np.ascontiguousarray(np.asarray(L0_attn_qkv_weights, dtype=np.float32))
    wm = np.ascontiguousarray(np.asarray(L0_wmod, dtype=np.float32))
    wu = np.ascontiguousarray(np.asarray(L0_ffn_w_up, dtype=np.float32))
    wg = np.ascontiguousarray(np.asarray(L0_ffn_w_gate, dtype=np.float32))
    wd = np.ascontiguousarray(np.asarray(L0_ffn_w_down, dtype=np.float32))
    cc = np.asarray(L0_c, dtype=np.float32)

    in_maps = []
    ident_np = np.eye(128, dtype=np.float32)
    for c in range(8):
        b, par = c // 2, c % 2
        xT_ = np.ascontiguousarray(x[b].T)
        xoT_ = np.ascontiguousarray(x[b, par::2].T)
        cvec_ = np.ascontiguousarray(cc[b].reshape(8, 128).T)
        kl = np.arange(128)
        qg = 2 * np.arange(64) + par
        dm = np.where(kl[:, None] <= qg[None, :], 0.0, NEG).astype(np.float32)
        in_maps.append({
            "xT": xT_, "xoT": xoT_, "wqkv": wq, "wmod": wm, "cvec": cvec_,
            "wup_n": wu, "wgate_n": wg, "wdn_n": wd,
            "dmask": dm, "eshift": eshift_np, "ident": ident_np,
        })

    res = run_bass_kernel_spmd(nc, in_maps, core_ids=list(range(8)))
    outp = np.empty_like(x)
    for c in range(8):
        b, par = c // 2, c % 2
        outp[b, par::2, :] = res.results[c]["out"].T
    return outp
